# version 02
# baseline (speedup 1.0000x reference)
import contextlib
import numpy as np
import concourse.bass as bass
import concourse.mybir as mybir
from concourse.bass import IndirectOffsetOnAxis
from concourse.bass_utils import run_bass_kernel_spmd

F32 = mybir.dt.float32
BF16 = mybir.dt.bfloat16
U32 = mybir.dt.uint32
AF = mybir.ActivationFunctionType
ALU = mybir.AluOpType
AX = mybir.AxisListType

NCORES = 8
SAFE_STAGE = False
D = 1024
TOK = 2048
G = D // 128
LN_EPS = 1e-5
FSUB = 512




def dchain(eng, S, name, *, out, in_, **kw):
    eng.wait_ge(S.get(name), S.val(name))
    instr = eng.dma_start(out=out, in_=in_, **kw)
    instr.then_inc(S.get(name), 16)
    S.bump(name, 16)
    return instr


def ichain(eng, S, name, **kw):
    eng.wait_ge(S.get(name), S.val(name))
    instr = eng.indirect_dma_start(**kw)
    instr.then_inc(S.get(name), 16)
    S.bump(name, 16)
    return instr


class Sems:
    def __init__(self, nc, stack):
        self.nc = nc
        self.stack = stack
        self.h = {}
        self.c = {}

    def get(self, name):
        if name not in self.h:
            self.h[name] = self.stack.enter_context(self.nc.semaphore(name))
            self.c[name] = 0
        return self.h[name]

    def bump(self, name, k=1):
        self.get(name)
        self.c[name] += k
        return self.c[name]

    def val(self, name):
        self.get(name)
        return self.c[name]


class Bufs:

    def __init__(self, nc, stack):
        ec = stack.enter_context
        self.PS = ec(nc.psum_tensor("PS", [128, 8, FSUB], F32))
        self.xsub = ec(nc.sbuf_tensor("xsub", [128, G, FSUB], F32))
        self.xb = ec(nc.sbuf_tensor("xb", [128, G, FSUB], BF16))
        self.sqb = ec(nc.sbuf_tensor("sqb", [128, G, FSUB], BF16))
        self.rows = ec(nc.sbuf_tensor("rows", [1, 6 * FSUB], F32))
        self.ones128_bf = ec(nc.sbuf_tensor("ones128_bf", [128, 1], BF16))
        self.ones_row_f = ec(nc.sbuf_tensor("ones_row_f", [1, 128], F32))
        self.u = ec(nc.sbuf_tensor("u", [128, 32, 1024], BF16))
        self.v = ec(nc.sbuf_tensor("v", [128, 32, 1024], BF16))
        self.panu = ec(nc.sbuf_tensor("panu", [128, 4, G, 128], BF16))
        self.panh = ec(nc.sbuf_tensor("panh", [128, 2, 32, 128], BF16))
        self.st = ec(nc.sbuf_tensor("st", [128, 2, FSUB], F32))
        self.xn = self.v[:, 0:G, :]
        self.uf32 = self.u[:].rearrange("p a c -> p (a c)").bitcast(F32).rearrange("p (g c) -> p g c", g=G)
        self.vf32 = self.v[:].rearrange("p a c -> p (a c)").bitcast(F32).rearrange("p (g c) -> p g c", g=G)
        self.xsub_flat = self.xsub[:].rearrange("p g c -> p (g c)")
        self.sqb_i16 = self.sqb[:].rearrange("p g c -> p (g c)").bitcast(mybir.dt.int16)
        self.b03_last = []
        self.b47_last = []
        self.pan_last = {}
        self.scr_guard = []

    def row(self, i):
        return self.rows[0:1, i * FSUB:(i + 1) * FSUB]

    def init_consts(self, block, S):
        @block.vector
        def _(v):
            v.memset(self.ones128_bf[:], 1.0)
            v.memset(self.ones_row_f[:], 1.0).then_inc(S.get("init"), 1)
        S.bump("init", 1)


def fm(ap2d, c0, w):
    return ap2d[:, c0:c0 + w].rearrange("(g p) c -> p g c", p=128)



def emit_norm(block, nc, S, B, *, src_dram=None, src_sbuf=None, ft, out,
              out_col0=0, mean=True, eps=LN_EPS, scale_tile=None,
              gamma=None, beta=None, bank_guards=(), src_ready=None,
              double=False, apply=True, subs=(0, 1), xsub_guards=()):
    PS = B.PS
    for sub in subs:
        c0 = ft * 1024 + sub * FSUB
        lc0 = sub * FSUB

        guards = tuple(B.scr_guard)
        B.scr_guard = []
        if src_sbuf is None:
            need_xnd = S.val("xnd")
            need_out = S.val("outd")
            t_xld = S.bump("xld", 16)

            @block.sync
            def _(q, c0=c0, need_xnd=need_xnd, need_out=need_out, t_xld=t_xld,
                  xg=tuple(xsub_guards)):
                q.wait_ge(S.get("xnd"), need_xnd)
                q.wait_ge(S.get("outd"), need_out)
                for nm, vv in xg:
                    q.wait_ge(S.get(nm), vv)
                q.wait_ge(S.get("xld"), t_xld - 16)
                q.dma_start(out=B.xsub[:], in_=fm(src_dram, c0, FSUB)).then_inc(S.get("xld"), 16)

            xs = B.xsub
            xs_g = lambda g: xs[:, g, :]
            if mean:
                t_xbl = S.bump("xbl", 16)

                @block.gpsimd
                def _(g, c0=c0, need_xnd=need_xnd, guards=guards, t_xbl=t_xbl):
                    g.wait_ge(S.get("xnd"), need_xnd)
                    for nm, vv in guards:
                        g.wait_ge(S.get(nm), vv)
                    g.wait_ge(S.get("xbl"), t_xbl - 16)
                    g.dma_start(out=B.xb[:], in_=fm(src_dram, c0, FSUB)).then_inc(S.get("xbl"), 16)
        else:
            xs_g = lambda g, c0=c0: src_sbuf[:, g, c0:c0 + FSUB]
            t_xld = S.val("xld")
            if mean:
                need_xnd = S.val("xnd")

                @block.scalar
                def _(s, c0=c0, need_xnd=need_xnd):
                    s.wait_ge(S.get("xnd"), need_xnd)
                    if src_ready is not None:
                        s.wait_ge(S.get(src_ready[0]), src_ready[1])
                    for g in range(G):
                        last = s.copy(B.xb[:, g, :], src_sbuf[:, g, c0:c0 + FSUB])
                    last.then_inc(S.get("xbl"), 1)
                t_xbl = S.bump("xbl", 1)

        @block.scalar
        def _(s, c0=c0, guards=guards):
            for nm, vv in guards:
                s.wait_ge(S.get(nm), vv)
            if src_ready is not None:
                s.wait_ge(S.get(src_ready[0]), src_ready[1])
            if mean:
                s.wait_ge(S.get("xbl"), S.val("xbl"))
                src_sq = lambda g: B.xb[:, g, :]
            else:
                if src_sbuf is None:
                    s.wait_ge(S.get("xld"), S.val("xld"))
                src_sq = xs_g
            for g in range(G):
                last = s.activation(B.sqb[:, g, :], src_sq(g), AF.Square)
            last.then_inc(S.get("sqd"), 1)
        t_sqd = S.bump("sqd", 1)

        need_statb = S.val("statb")
        bg = list(bank_guards) + list(B.b47_last)
        B.b47_last = []

        @block.tensor
        def _(t, t_sqd=t_sqd, need_statb=need_statb, bg=tuple(bg)):
            t.wait_ge(S.get("statb"), need_statb)
            for nm, vv in bg:
                t.wait_ge(S.get(nm), vv)
            if mean:
                t.wait_ge(S.get("xbl"), S.val("xbl"))
                for g in range(G):
                    mm = t.matmul(PS[0:1, 4, :], B.ones128_bf[:], B.xb[:, g, :],
                                  start=(g == 0), stop=(g == G - 1))
                mm.then_inc(S.get("sums"), 1)
            t.wait_ge(S.get("sqd"), t_sqd)
            for g in range(G):
                mm = t.matmul(PS[0:1, 5, :], B.ones128_bf[:], B.sqb[:, g, :],
                              start=(g == 0), stop=(g == G - 1))
            mm.then_inc(S.get("sums"), 1)
        t_sums = S.bump("sums", 2 if mean else 1)

        @block.vector
        def _(v, t_sums=t_sums):
            v.wait_ge(S.get("sums"), t_sums)
            mu, m2, va = B.row(0), B.row(1), B.row(2)
            vstep(v, S, v.tensor_scalar(va, PS[0:1, 5, :], 1.0 / D, None, op0=ALU.mult))
            if mean:
                vstep(v, S, v.tensor_scalar(mu, PS[0:1, 4, :], 1.0 / D, None, op0=ALU.mult))
                vstep(v, S, v.tensor_tensor(m2, mu, mu, op=ALU.mult))
                vstep(v, S, v.tensor_tensor(va, va, m2, op=ALU.subtract))
            v.tensor_scalar(va, va, eps, None, op0=ALU.add).then_inc(S.get("statv"), 1)
        t_statv = S.bump("statv", 1)

        @block.scalar
        def _(s, t_statv=t_statv):
            s.wait_ge(S.get("statv"), t_statv)
            va, ln, arow = B.row(2), B.row(3), B.row(4)
            t_ach = S.bump("acch", 1)
            s.activation(ln, va, AF.Ln).then_inc(S.get("acch"), 1)
            s.wait_ge(S.get("acch"), t_ach)
            s.activation(arow, ln, AF.Exp, scale=-0.5).then_inc(S.get("stata"), 1)
        t_stata = S.bump("stata", 1)

        if double:
            @block.vector
            def _(v, t_stata=t_stata):
                v.wait_ge(S.get("stata"), t_stata)
                vstep(v, S, v.tensor_tensor(B.row(1), B.row(4), B.row(4),
                                            op=ALU.mult))
                v.tensor_scalar(B.row(1), B.row(1), -eps, 1.0 + eps,
                                op0=ALU.mult, op1=ALU.add).then_inc(S.get("dln"), 1)
            t_dln = S.bump("dln", 1)

            @block.scalar
            def _(s, t_dln=t_dln):
                s.wait_ge(S.get("dln"), t_dln)
                t_ach2 = S.bump("acch", 1)
                s.activation(B.row(3), B.row(1), AF.Ln).then_inc(S.get("acch"), 1)
                s.wait_ge(S.get("acch"), t_ach2)
                s.activation(B.row(1), B.row(3), AF.Exp,
                             scale=-0.5).then_inc(S.get("stata"), 1)
            t_stata = S.bump("stata", 1)

        @block.vector
        def _(v, t_stata=t_stata):
            v.wait_ge(S.get("stata"), t_stata)
            if double:
                vstep(v, S, v.tensor_tensor(B.row(4), B.row(4), B.row(1),
                                            op=ALU.mult))
            if mean:
                v.tensor_tensor(B.row(5), B.row(0), B.row(4), op=ALU.mult
                                ).then_inc(S.get("statb"), 1)
            else:
                v.tensor_copy(B.row(5), B.row(4)).then_inc(S.get("statb"), 1)
        t_statb = S.bump("statb", 1)

        need_xnd2 = S.val("xnd")

        @block.tensor
        def _(t, t_statb=t_statb, need_xnd2=need_xnd2):
            t.wait_ge(S.get("xnd"), need_xnd2)
            t.wait_ge(S.get("statb"), t_statb)
            mm = t.matmul(PS[:, 6, :], B.ones_row_f[:], B.row(4),
                          start=True, stop=True)
            mm.then_inc(S.get("bcd"), 1)
            if mean:
                t.matmul(PS[:, 7, :], B.ones_row_f[:], B.row(5),
                         start=True, stop=True).then_inc(S.get("bcd"), 1)
        t_bcd = S.bump("bcd", 2 if mean else 1)

        if not apply:
            return t_bcd, t_xld

        @block.vector
        def _(v, t_bcd=t_bcd, lc0=lc0, t_xld=t_xld):
            v.wait_ge(S.get("bcd"), t_bcd)
            if src_sbuf is None:
                v.wait_ge(S.get("xld"), t_xld)
            for g in range(G):
                dst = out[:, g, out_col0 + lc0:out_col0 + lc0 + FSUB]
                tmp = B.st[:, 0, :]
                if mean:
                    vstep(v, S, v.tensor_tensor(tmp, xs_g(g), PS[:, 6, :], op=ALU.mult))
                    if gamma is None:
                        last = v.tensor_tensor(dst, tmp, PS[:, 7, :], op=ALU.subtract)
                    else:
                        vstep(v, S, v.tensor_tensor(tmp, tmp, PS[:, 7, :], op=ALU.subtract))
                        last = v.tensor_scalar(dst, tmp, gamma[:, g:g + 1],
                                               beta[:, g:g + 1], op0=ALU.mult, op1=ALU.add)
                    if g < G - 1:
                        vstep(v, S, last)
                else:
                    if scale_tile is None:
                        last = v.tensor_tensor(dst, xs_g(g), PS[:, 6, :], op=ALU.mult)
                    else:
                        last = v.scalar_tensor_tensor(dst, xs_g(g),
                                                      scale_tile[:, g:g + 1], PS[:, 6, :],
                                                      op0=ALU.mult, op1=ALU.mult)
            last.then_inc(S.get("xnd"), 1)
        S.bump("xnd", 1)
        B.b47_last = [("xnd", S.val("xnd"))]


def emit_ln(block, nc, S, B, x_dram, ft, xn_out, eps=LN_EPS):
    emit_norm(block, nc, S, B, src_dram=x_dram, ft=ft, out=xn_out, mean=True, eps=eps)



def emit_stage(block, nc, S, B, *, tag, wT_dram, pan, pan_nbuf, n_og, n_kg,
               rhs_tile, rhs_cols0, writer, n_fsub=2, bank_rot=2, wait_before=(),
               pan_key=None):
    PS = B.PS
    pan_guards = []
    if pan_key is not None and pan_key in B.pan_last:
        pg, pgv = B.pan_last[pan_key]
        pan_guards.append((pg, pgv))
    wait_before = list(wait_before) + list(B.b03_last)
    B.b03_last = []
    accs = S.get("acc_" + tag)
    evs = S.get("ev_" + tag)
    base_acc = S.val("acc_" + tag)
    base_ev = S.val("ev_" + tag)
    nbuf = pan_nbuf
    wsl = [S.get("w_%s_%d" % (tag, s)) for s in range(nbuf)]
    base_wsl = [S.val("w_%s_%d" % (tag, s)) for s in range(nbuf)]

    def load_panel(g, j):
        src = wT_dram[:, j * 128:(j + 1) * 128].rearrange("(g p) c -> p g c", p=128)
        g.wait_ge(wsl[j % nbuf], S.val("w_%s_%d" % (tag, j % nbuf)))
        g.dma_start(out=pan[:, j % nbuf, 0:n_kg, :], in_=src).then_inc(wsl[j % nbuf], 16)
        S.bump("w_%s_%d" % (tag, j % nbuf), 16)

    for og in range(n_og):
        @block.gpsimd
        def _(g, og=og):
            if SAFE_STAGE:
                if og == 0:
                    for nm, vv in pan_guards:
                        g.wait_ge(S.get(nm), vv)
                g.wait_ge(accs, base_acc + og * n_fsub)
                g.wait_ge(S.get("ev_" + tag), base_ev + og * n_fsub)
                load_panel(g, og)
                return
            if og == 0:
                for nm, vv in pan_guards:
                    g.wait_ge(S.get(nm), vv)
                for j in range(min(nbuf, n_og)):
                    load_panel(g, j)
            else:
                j = og + nbuf - 1
                if j < n_og:
                    g.wait_ge(accs, base_acc + (j - nbuf + 1) * n_fsub)
                    load_panel(g, j)

        @block.tensor
        def _(t, og=og):
            t.wait_ge(wsl[og % nbuf], base_wsl[og % nbuf] + 16 * (og // nbuf + 1))
            if og == 0:
                for nm, vv in wait_before:
                    t.wait_ge(S.get(nm), vv)
            for fsub in range(n_fsub):
                bank = (og % bank_rot) * n_fsub + fsub
                if og >= bank_rot:
                    t.wait_ge(evs, base_ev + (og - bank_rot) * n_fsub + fsub + 1)
                cs = slice(rhs_cols0 + fsub * FSUB, rhs_cols0 + (fsub + 1) * FSUB)
                for k in range(n_kg):
                    mm = t.matmul(PS[:, bank, :], pan[:, og % nbuf, k, :],
                                  rhs_tile[:, k, cs],
                                  start=(k == 0), stop=(k == n_kg - 1))
                mm.then_inc(accs, 1)
        for fsub in range(n_fsub):
            t_acc = S.bump("acc_" + tag, 1)
            bank = (og % bank_rot) * n_fsub + fsub
            writer(block, og, fsub, bank, t_acc)
            S.bump("ev_" + tag, 1)
    B.b03_last = [("ev_" + tag, S.val("ev_" + tag))]
    if pan_key is not None:
        B.pan_last[pan_key] = ("acc_" + tag, S.val("acc_" + tag))


def act_writer(S, B, tag, dst, scale_tile, func, dst_cols0=0):
    accs_name, evs_name = "acc_" + tag, "ev_" + tag

    def writer(block, og, fsub, bank, t_acc):
        @block.scalar
        def _(s, og=og, fsub=fsub, bank=bank, t_acc=t_acc):
            s.wait_ge(S.get(accs_name), t_acc)
            kw = {}
            if scale_tile is not None:
                kw["scale"] = scale_tile[:, og:og + 1]
            s.activation(dst[:, og, dst_cols0 + fsub * FSUB:dst_cols0 + (fsub + 1) * FSUB],
                         B.PS[:, bank, :], func, **kw).then_inc(S.get(evs_name), 1)
    return writer



def emit_l0ffn(block, nc, S, B, W, x_dram, xout_dram):
    PS = B.PS
    for ft in range(2):
        emit_ln(block, nc, S, B, x_dram, ft, B.xn)
        t_xn = S.val("xnd")
        cols0 = ft * 1024

        accs = S.get("acc_u"); evs = S.get("ev_u")
        base_acc_u = S.val("acc_u"); base_ev = S.val("ev_u")
        NBUF = 2
        uwsl = [[S.get("w_u_%d%d" % (s, wi)) for wi in range(2)] for s in range(NBUF)]
        base_uwsl = [[S.val("w_u_%d%d" % (s, wi)) for wi in range(2)] for s in range(NBUF)]

        def load_pair(g, j):
            for wi, wT in enumerate((W["w1T"], W["w2T"])):
                src = wT[:, j * 128:(j + 1) * 128].rearrange("(g p) c -> p g c", p=128)
                g.wait_ge(uwsl[j % NBUF][wi], S.val("w_u_%d%d" % (j % NBUF, wi)))
                g.dma_start(out=B.panu[:, (j % NBUF) * 2 + wi, :, :], in_=src
                            ).then_inc(uwsl[j % NBUF][wi], 16)
                S.bump("w_u_%d%d" % (j % NBUF, wi), 16)

        u_guards = list(B.b03_last) + list(B.b47_last)
        B.b03_last = []
        B.b47_last = []
        u_pan_guard = B.pan_last.get("panu")
        for og in range(32):
            @block.gpsimd
            def _(g, og=og):
                if og == 0:
                    if u_pan_guard is not None:
                        g.wait_ge(S.get(u_pan_guard[0]), u_pan_guard[1])
                    for j in range(NBUF):
                        load_pair(g, j)
                else:
                    j = og + NBUF - 1
                    if j < 32:
                        g.wait_ge(accs, base_acc_u + (j - NBUF + 1) * 4)
                        load_pair(g, j)

            @block.tensor
            def _(t, og=og):
                for wi in range(2):
                    t.wait_ge(uwsl[og % NBUF][wi],
                              base_uwsl[og % NBUF][wi] + 16 * (og // NBUF + 1))
                if og == 0:
                    t.wait_ge(S.get("xnd"), t_xn)
                    for nm, vv in u_guards:
                        t.wait_ge(S.get(nm), vv)
                for fsub in range(2):
                    ba = (og % 2) * 4 + fsub * 2
                    bb = ba + 1
                    if og >= 2:
                        t.wait_ge(evs, base_ev + (og - 2) * 2 + fsub + 1)
                    cs = slice(fsub * FSUB, (fsub + 1) * FSUB)
                    for k in range(G):
                        mm = t.matmul(PS[:, ba, :], B.panu[:, (og % NBUF) * 2, k, :],
                                      B.xn[:, k, cs], start=(k == 0), stop=(k == G - 1))
                    mm.then_inc(accs, 1)
                    for k in range(G):
                        mm = t.matmul(PS[:, bb, :], B.panu[:, (og % NBUF) * 2 + 1, k, :],
                                      B.xn[:, k, cs], start=(k == 0), stop=(k == G - 1))
                    mm.then_inc(accs, 1)

            for fsub in range(2):
                t_acc = S.bump("acc_u", 2)
                ba = (og % 2) * 4 + fsub * 2
                bb = ba + 1
                need_ev = base_ev + 2 * (og - 1) + fsub + 1 if og >= 1 else base_ev

                @block.scalar
                def _(s, og=og, fsub=fsub, ba=ba, t_acc=t_acc, need_ev=need_ev):
                    s.wait_ge(evs, need_ev)
                    s.wait_ge(accs, t_acc - 1)
                    s.activation(B.st[:, fsub, :], PS[:, ba, :], AF.Silu,
                                 scale=W["s1"][:, og:og + 1]).then_inc(S.get("sil_u"), 1)
                t_sil = S.bump("sil_u", 1)

                @block.vector
                def _(v, og=og, fsub=fsub, bb=bb, t_acc=t_acc, t_sil=t_sil):
                    v.wait_ge(accs, t_acc)
                    v.wait_ge(S.get("sil_u"), t_sil)
                    v.scalar_tensor_tensor(
                        B.u[:, og, fsub * FSUB:(fsub + 1) * FSUB], PS[:, bb, :],
                        W["s2"][:, og:og + 1], B.st[:, fsub, :],
                        op0=ALU.mult, op1=ALU.mult).then_inc(evs, 1)
                S.bump("ev_u", 1)
        t_u_done = S.val("ev_u")
        B.b03_last = [("ev_u", t_u_done)]
        B.b47_last = [("ev_u", t_u_done)]
        B.pan_last["panu"] = ("acc_u", S.val("acc_u"))

        emit_stage(block, nc, S, B, tag="v", wT_dram=W["h1T"], pan=B.panh,
                   pan_nbuf=2, n_og=32, n_kg=32, rhs_tile=B.u, rhs_cols0=0,
                   writer=act_writer(S, B, "v", B.v, W["sh1"], AF.Gelu),
                   bank_rot=2, wait_before=[("ev_u", t_u_done)], pan_key="panh")
        t_v_done = S.val("ev_v")

        emit_stage(block, nc, S, B, tag="w", wT_dram=W["h2T"], pan=B.panh,
                   pan_nbuf=2, n_og=32, n_kg=32, rhs_tile=B.v, rhs_cols0=0,
                   writer=act_writer(S, B, "w", B.u, W["sh2"], AF.Gelu),
                   bank_rot=2, wait_before=[("ev_v", t_v_done)], pan_key="panh")
        t_w_done = S.val("ev_w")

        for fsub in range(2):
            c0 = cols0 + fsub * FSUB
            need_xnd = S.val("xnd")
            need_out = S.val("outd")
            t_xld = S.bump("xld", 16)
            tag = "o%d" % fsub

            @block.sync
            def _(q, c0=c0, need_xnd=need_xnd, need_out=need_out, t_xld=t_xld):
                q.wait_ge(S.get("xnd"), need_xnd)
                q.wait_ge(S.get("outd"), need_out)
                q.wait_ge(S.get("xld"), t_xld - 16)
                q.dma_start(out=B.xsub[:], in_=fm(x_dram, c0, FSUB)).then_inc(S.get("xld"), 16)

            def wo(blk, og, fsub_i, bank, t_acc, tag=tag, t_xld=t_xld):
                @blk.vector
                def _(v, og=og, bank=bank, t_acc=t_acc):
                    v.wait_ge(S.get("acc_" + tag), t_acc)
                    v.wait_ge(S.get("xld"), t_xld)
                    vstep(v, S, v.tensor_scalar(B.st[:, 0, :], B.PS[:, bank, :],
                                                W["s3"][:, og:og + 1], 0.5,
                                                op0=ALU.mult, op1=ALU.mult))
                    v.tensor_tensor(B.xsub[:, og, :], B.xsub[:, og, :], B.st[:, 0, :],
                                    op=ALU.add).then_inc(S.get("ev_" + tag), 1)

            emit_stage(block, nc, S, B, tag=tag, wT_dram=W["w3T"],
                       pan=B.panh, pan_nbuf=2, n_og=8, n_kg=32, rhs_tile=B.u,
                       rhs_cols0=fsub * FSUB, writer=wo, n_fsub=1, bank_rot=4,
                       wait_before=[("ev_w", t_w_done)], pan_key="panh")
            t_res = S.val("ev_" + tag)

            t_od = S.val("outd")

            @block.sync
            def _(q, c0=c0, t_res=t_res, tag=tag, t_od=t_od):
                q.wait_ge(S.get("ev_" + tag), t_res)
                q.wait_ge(S.get("outd"), t_od)
                q.dma_start(out=fm(xout_dram, c0, FSUB), in_=B.xsub[:]).then_inc(S.get("outd"), 16)
            S.bump("outd", 16)



def l0ffn_host_weights(inp):
    f = np.float32
    return {
        "w1T": np.ascontiguousarray(np.asarray(inp["L0_ffn_w1_w"], f).T),
        "w2T": np.ascontiguousarray(np.asarray(inp["L0_ffn_w2_w"], f).T),
        "h1T": np.ascontiguousarray(np.asarray(inp["L0_ffn_h1_w"], f).T),
        "h2T": np.ascontiguousarray(np.asarray(inp["L0_ffn_h2_w"], f).T),
        "w3T": np.ascontiguousarray(np.asarray(inp["L0_ffn_w3_w"], f).T),
        "s1": pack_scale(inp["L0_ffn_w1_s"], 32),
        "s2": pack_scale(inp["L0_ffn_w2_s"], 32),
        "sh1": pack_scale(inp["L0_ffn_h1_s"], 32),
        "sh2": pack_scale(inp["L0_ffn_h2_s"], 32),
        "s3": pack_scale(inp["L0_ffn_w3_s"], 8),
    }


def declare_l0ffn_weights(nc, prefix=""):
    d = {}
    d["w1T"] = nc.dram_tensor(prefix + "w1T", [1024, 4096], F32, kind="ExternalInput").ap()
    d["w2T"] = nc.dram_tensor(prefix + "w2T", [1024, 4096], F32, kind="ExternalInput").ap()
    d["h1T"] = nc.dram_tensor(prefix + "h1T", [4096, 4096], F32, kind="ExternalInput").ap()
    d["h2T"] = nc.dram_tensor(prefix + "h2T", [4096, 4096], F32, kind="ExternalInput").ap()
    d["w3T"] = nc.dram_tensor(prefix + "w3T", [4096, 1024], F32, kind="ExternalInput").ap()
    for nm, ng in [("s1", 32), ("s2", 32), ("sh1", 32), ("sh2", 32), ("s3", 8)]:
        d[nm + "_dram"] = nc.dram_tensor(prefix + nm, [128, ng], F32, kind="ExternalInput").ap()
    return d


def emit_load_scales(block, nc, S, B, stack, wd, names_groups, prefix=""):
    for nm, ng in names_groups:
        t = stack.enter_context(nc.sbuf_tensor(prefix + nm + "_sb", [128, ng], F32))
        wd[nm] = t

        tw = S.val("xld")

        @block.sync
        def _(q, t=t, src=wd[nm + "_dram"], tw=tw):
            q.wait_ge(S.get("xld"), tw)
            q.dma_start(out=t[:], in_=src).then_inc(S.get("xld"), 16)
        S.bump("xld", 16)


def build_l0ffn(debug=False):
    nc = bass.Bass("TRN2", target_bir_lowering=False, debug=debug)
    x = nc.dram_tensor("x", [D, TOK], F32, kind="ExternalInput").ap()
    xout = nc.dram_tensor("xout", [D, TOK], F32, kind="ExternalOutput").ap()
    wd = declare_l0ffn_weights(nc)
    with contextlib.ExitStack() as stack:
        S = Sems(nc, stack)
        B = Bufs(nc, stack)
        block = stack.enter_context(nc.Block())
        B.init_consts(block, S)
        emit_load_scales(block, nc, S, B, stack, wd,
                         [("s1", 32), ("s2", 32), ("sh1", 32), ("sh2", 32), ("s3", 8)])
        t_init = S.val("init")
        t_scales = S.val("xld")

        @block.tensor
        def _(t):
            t.wait_ge(S.get("init"), t_init)

        @block.scalar
        def _(s):
            s.wait_ge(S.get("xld"), t_scales)

        @block.vector
        def _(v):
            v.wait_ge(S.get("xld"), t_scales)

        emit_l0ffn(block, nc, S, B, wd, x, xout)

        t_done = S.val("outd")

        @block.sync
        def _(q):
            q.wait_ge(S.get("outd"), t_done)
    return nc


def pack_scale(s, ngroups):
    return np.ascontiguousarray(np.asarray(s, np.float32).reshape(ngroups, 128).T)


def fm_shards(x):
    xt = np.asarray(x, np.float32).reshape(-1, D)
    return [np.ascontiguousarray(xt[c * TOK:(c + 1) * TOK].T) for c in range(NCORES)]


def fm_unshard(shards):
    xt = np.concatenate([np.asarray(s).T for s in shards], 0)
    return xt.reshape(2, 8192, D)



def emit_l0attn(block, nc, S, B, W, x_dram, xout_dram, dbg=None, dbg_stop=False):
    PS = B.PS
    yv = B.uf32
    sv = B.vf32
    h_scr = B.xsub_flat[:, 0:2048]
    pow0 = B.xsub_flat[:, 0:2048]
    pow1 = B.xsub_flat[:, 2048:4096]
    iota16 = B.sqb_i16[:, 0:2048]

    for ft in range(2):
        emit_norm(block, nc, S, B, src_dram=x_dram, ft=ft, out=yv,
                  out_col0=ft * 1024, mean=True, eps=LN_EPS)
    t_y = S.val("xnd")
    for ft in range(2):
        emit_norm(block, nc, S, B, src_sbuf=yv, ft=ft, out=sv,
                  out_col0=ft * 1024, mean=False, eps=1e-6, scale_tile=W["iw"],
                  src_ready=("xnd", t_y))
    t_nm = S.val("xnd")

    @block.scalar
    def _(s):
        t_a1 = S.bump("acch", 1)
        s.activation(W["t0"][:], W["alog"][:], AF.Exp).then_inc(S.get("acch"), 1)
        s.wait_ge(S.get("acch"), t_a1)
        s.activation(W["acol"][:], W["t0"][:], AF.Exp, scale=-1.0)
        s.activation(W["ach"][:], W["t0"][:], AF.Exp, scale=-2048.0).then_inc(S.get("attc"), 1)
    t_attc = S.bump("attc", 1)

    @block.vector
    def _(v, t_attc=t_attc):
        v.wait_ge(S.get("attc"), t_attc)
        v.tensor_scalar(W["nt0"][:], W["t0"][:], -1.0, None, op0=ALU.mult).then_inc(S.get("attc"), 1)
    t_attc2 = S.bump("attc", 1)

    t_sums_b = S.val("sums")

    @block.gpsimd
    def _(g, t_sums_b=t_sums_b):
        g.wait_ge(S.get("sums"), t_sums_b)
        g.iota(iota16, pattern=[[1, 2048]], base=1, channel_multiplier=0).then_inc(S.get("iotd"), 1)
    t_iotd = S.bump("iotd", 1)

    need_out = S.val("outd")

    @block.vector
    def _(v, t_nm=t_nm, t_attc2=t_attc2, need_out=need_out):
        v.wait_ge(S.get("xnd"), t_nm)
        v.wait_ge(S.get("attc"), t_attc2)
        v.wait_ge(S.get("outd"), need_out)
        for g in range(G):
            t1 = S.bump("ssc", 1)
            v.tensor_tensor_scan(h_scr, W["acol"][:, g:g + 1].broadcast_to((128, TOK)),
                                 sv[:, g, :], initial=0.0, op0=ALU.mult,
                                 op1=ALU.add).then_inc(S.get("ssc"), 1)
            v.wait_ge(S.get("ssc"), t1)
            t2 = S.bump("ssc", 1)
            v.tensor_copy(W["states"][:, g:g + 1], h_scr[:, 2047:2048]).then_inc(S.get("ssc"), 1)
            v.wait_ge(S.get("ssc"), t2)
            vstep(v, S, v.tensor_scalar(h_scr, h_scr, W["cout"][:, g:g + 1], None,
                                        op0=ALU.mult))
            last = v.scalar_tensor_tensor(sv[:, g, :], sv[:, g, :], W["dskip"][:, g:g + 1],
                                          h_scr, op0=ALU.mult, op1=ALU.add)
        last.then_inc(S.get("scnd"), 1)
    t_scn = S.bump("scnd", 1)

    @block.gpsimd
    def _(g, t_scn=t_scn):
        g.wait_ge(S.get("scnd"), t_scn)
        g.wait_ge(S.get("csd"), S.val("csd"))
        g.dma_start(out=W["cstat_in"].ap(), in_=W["states"][:]).then_inc(S.get("csd"), 16)
        g.wait_ge(S.get("csd"), S.val("csd") + 16)
        g.collective_compute(
            "AllGather", ALU.bypass, replica_groups=[list(range(NCORES))],
            ins=[W["cstat_in"].ap().opt()], outs=[W["cstat_all"].ap().opt()],
        ).then_inc(S.get("ccl0"))
        g.wait_ge(S.get("ccl0"), 1)
        g.dma_start(out=W["sall"][:],
                    in_=W["cstat_all"].ap().rearrange("(j p) g -> p j g", p=128)
                    ).then_inc(S.get("csd"), 16)
    S.bump("csd", 32)
    S.bump("ccl0", 1)
    t_csd = S.val("csd")

    @block.vector
    def _(v, t_csd=t_csd):
        v.wait_ge(S.get("csd"), t_csd)

        def step(instr):
            t = S.bump("vch", 1)
            instr.then_inc(S.get("vch"), 1)
            v.wait_ge(S.get("vch"), t)
        for k in range(3):
            step(v.memset(W["Tk"][:, k, :], 0.0))
            for j in range(NCORES):
                step(v.scalar_tensor_tensor(W["Tk"][:, k, :], W["sall"][:, j, :],
                                            W["selcols"][:, k * 8 + j:k * 8 + j + 1],
                                            W["Tk"][:, k, :], op0=ALU.mult, op1=ALU.add))
        step(v.tensor_tensor(W["a2"][:], W["ach"][:], W["ach"][:], op=ALU.mult))
        step(v.tensor_tensor(W["E"][:], W["ach"][:], W["Tk"][:, 1, :], op=ALU.mult))
        step(v.tensor_tensor(W["E"][:], W["E"][:], W["Tk"][:, 0, :], op=ALU.add))
        step(v.tensor_tensor(W["chE"][:], W["a2"][:], W["Tk"][:, 2, :], op=ALU.mult))
        step(v.tensor_tensor(W["E"][:], W["E"][:], W["chE"][:], op=ALU.add))
        v.tensor_tensor(W["chE"][:], W["cout"][:], W["E"][:], op=ALU.mult).then_inc(S.get("exc"), 1)
    t_exc = S.bump("exc", 1)

    for g in range(G):
        pw = pow0 if g % 2 == 0 else pow1
        need_cor = S.val("cord") - 1 if g >= 2 else 0

        @block.scalar
        def _(s, g=g, pw=pw, need_cor=need_cor, t_iotd=S.val("iotd"), t_exc=t_exc):
            if g == 0:
                s.wait_ge(S.get("iotd"), t_iotd)
                s.wait_ge(S.get("exc"), t_exc)
            if g >= 2:
                s.wait_ge(S.get("cord"), need_cor + 1)
            s.activation(pw, iota16, AF.Exp, scale=W["nt0"][:, g:g + 1]).then_inc(S.get("powd"), 1)
        t_pow = S.bump("powd", 1)

        @block.vector
        def _(v, g=g, pw=pw, t_pow=t_pow, t_exc=t_exc):
            v.wait_ge(S.get("powd"), t_pow)
            if g == 0:
                v.wait_ge(S.get("exc"), t_exc)
            v.scalar_tensor_tensor(sv[:, g, :], pw, W["chE"][:, g:g + 1], sv[:, g, :],
                                   op0=ALU.mult, op1=ALU.add).then_inc(S.get("cord"), 1)
        S.bump("cord", 1)
    t_cor = S.val("cord")

    @block.vector
    def _(v, t_cor=t_cor):
        v.wait_ge(S.get("cord"), t_cor)
        for g in range(G):
            last = v.tensor_tensor(yv[:, g, :], yv[:, g, :], sv[:, g, :], op=ALU.add)
        last.then_inc(S.get("x1d"), 1)
    t_x1 = S.bump("x1d", 1)

    if dbg is not None:
        @block.sync
        def _(q, t_x1=t_x1):
            q.wait_ge(S.get("x1d"), t_x1)
            q.dma_start(out=dbg["x1"].rearrange("(g p) c -> p g c", p=128),
                        in_=yv[:, :, :]).then_inc(S.get("dbgd"), 16)
            q.dma_start(out=dbg["so"].rearrange("(g p) c -> p g c", p=128),
                        in_=sv[:, :, :]).then_inc(S.get("dbgd"), 16)
            q.dma_start(out=dbg["E"], in_=W["E"][:]).then_inc(S.get("dbgd"), 16)
            q.dma_start(out=dbg["chE"], in_=W["chE"][:]).then_inc(S.get("dbgd"), 16)
            q.dma_start(out=dbg["states"], in_=W["states"][:]).then_inc(S.get("dbgd"), 16)
            q.dma_start(out=dbg["sall"], in_=W["sall"][:]).then_inc(S.get("dbgd"), 16)
            q.wait_ge(S.get("dbgd"), S.val("dbgd") + 96)
        S.bump("dbgd", 96)
        if dbg_stop:
            return

    n2v = B.v[:].rearrange("p a c -> p (a c)")[:, 0:16384].rearrange(
        "p (g c) -> p g c", g=G)
    hidv = B.v[:, 16:32, :]
    for ft in range(2):
        cols0 = ft * 1024
        emit_norm(block, nc, S, B, src_sbuf=yv, ft=ft, out=n2v,
                  out_col0=cols0, mean=False, eps=1e-6, scale_tile=W["fnw"],
                  bank_guards=[("x1d", t_x1)], src_ready=("x1d", t_x1))
        t_n2 = S.val("xnd")

        emit_stage(block, nc, S, B, tag="hid", wT_dram=W["fcT"], pan=B.panu,
                   pan_nbuf=2, n_og=16, n_kg=G, rhs_tile=n2v, rhs_cols0=cols0,
                   writer=act_writer(S, B, "hid", hidv, None, AF.Silu),
                   bank_rot=2, wait_before=[("xnd", t_n2)], pan_key="panu")
        t_hid = S.val("ev_hid")

        for fsub in range(2):
            c0 = cols0 + fsub * FSUB
            need_xnd = S.val("xnd")
            need_out = S.val("outd")
            t_xld = S.bump("xld", 16)
            tag = "po%d" % fsub

            @block.sync
            def _(q, c0=c0, need_xnd=need_xnd, need_out=need_out, t_xld=t_xld):
                q.wait_ge(S.get("xnd"), need_xnd)
                q.wait_ge(S.get("outd"), need_out)
                q.wait_ge(S.get("cord"), S.val("cord"))
                q.wait_ge(S.get("xld"), t_xld - 16)
                q.dma_start(out=B.xsub[:], in_=fm(x_dram, c0, FSUB)).then_inc(S.get("xld"), 16)

            def wo(blk, og, fsub_i, bank, t_acc, tag=tag, t_xld=t_xld, lc0=fsub * FSUB + cols0):
                @blk.vector
                def _(v, og=og, bank=bank, t_acc=t_acc, lc0=lc0):
                    v.wait_ge(S.get("acc_" + tag), t_acc)
                    v.wait_ge(S.get("xld"), t_xld)
                    vstep(v, S, v.tensor_tensor(B.st[:, 0, :], B.PS[:, bank, :],
                                                yv[:, og, lc0:lc0 + FSUB], op=ALU.add))
                    v.tensor_tensor(B.xsub[:, og, :], B.xsub[:, og, :], B.st[:, 0, :],
                                    op=ALU.add).then_inc(S.get("ev_" + tag), 1)

            emit_stage(block, nc, S, B, tag=tag, wT_dram=W["projT"],
                       pan=B.panh, pan_nbuf=2, n_og=8, n_kg=16, rhs_tile=hidv,
                       rhs_cols0=fsub * FSUB, writer=wo, n_fsub=1, bank_rot=4,
                       wait_before=[("ev_hid", t_hid)], pan_key="panh")
            t_res = S.val("ev_" + tag)

            t_od = S.val("outd")

            @block.sync
            def _(q, c0=c0, t_res=t_res, tag=tag, t_od=t_od):
                q.wait_ge(S.get("ev_" + tag), t_res)
                q.wait_ge(S.get("outd"), t_od)
                q.dma_start(out=fm(xout_dram, c0, FSUB), in_=B.xsub[:]).then_inc(S.get("outd"), 16)
            S.bump("outd", 16)

    if dbg is not None and "n2" in dbg:
        @block.sync
        def _(q):
            q.wait_ge(S.get("outd"), S.val("outd"))
            q.dma_start(out=dbg["n2"].rearrange("(g p) c -> p g c", p=128),
                        in_=n2v[:, :, :]).then_inc(S.get("dbgd"), 16)
            q.dma_start(out=dbg["hid"], in_=hidv[:, :, :]).then_inc(S.get("dbgd"), 16)
            q.wait_ge(S.get("dbgd"), S.val("dbgd") + 32)
        S.bump("dbgd", 32)


def declare_l0attn_weights(nc, stack, S, block, B, prefix="a_"):
    d = {}
    d["fcT"] = nc.dram_tensor(prefix + "fcT", [1024, 2048], F32, kind="ExternalInput").ap()
    d["projT"] = nc.dram_tensor(prefix + "projT", [2048, 1024], F32, kind="ExternalInput").ap()
    for nm, ng in [("iw", G), ("alog", G), ("cout", G), ("dskip", G), ("fnw", G),
                   ("selcols", 24)]:
        d[nm + "_dram"] = nc.dram_tensor(prefix + nm, [128, ng], F32, kind="ExternalInput").ap()
        t = stack.enter_context(nc.sbuf_tensor(prefix + nm + "_sb", [128, ng], F32))
        d[nm] = t

        tw = S.val("xld")

        @block.sync
        def _(q, t=t, src=d[nm + "_dram"], tw=tw):
            q.wait_ge(S.get("xld"), tw)
            q.dma_start(out=t[:], in_=src).then_inc(S.get("xld"), 16)
        S.bump("xld", 16)
    ec = stack.enter_context
    for nm, w in [("t0", G), ("nt0", G), ("acol", G), ("ach", G), ("a2", G),
                  ("states", G), ("E", G), ("chE", G)]:
        d[nm] = ec(nc.sbuf_tensor(prefix + nm + "_t", [128, w], F32))
    d["sall"] = ec(nc.sbuf_tensor(prefix + "sall", [128, NCORES, G], F32))
    d["Tk"] = ec(nc.sbuf_tensor(prefix + "Tk", [128, 3, G], F32))
    d["cstat_in"] = nc.dram_tensor(prefix + "cstat_in", [128, G], F32)
    d["cstat_all"] = nc.dram_tensor(prefix + "cstat_all", [128 * NCORES, G], F32)
    return d


def l0attn_host_weights(inp, core):
    f = np.float32
    b, qc = core // 4, core % 4
    sel = np.zeros((3, 8), f)
    for k in range(3):
        j = qc - 1 - k
        if j >= 0:
            sel[k, b * 4 + j] = 1.0
    return {
        "fcT": np.ascontiguousarray(np.asarray(inp["L0_attn_ff_fc_w"], f).T),
        "projT": np.ascontiguousarray(np.asarray(inp["L0_attn_ff_proj_w"], f).T),
        "iw": pack_scale(inp["L0_attn_input_norm_w"], G),
        "alog": pack_scale(inp["L0_attn_a_log"], G),
        "cout": pack_scale(inp["L0_attn_c_out"], G),
        "dskip": pack_scale(inp["L0_attn_d_skip"], G),
        "fnw": pack_scale(inp["L0_attn_ff_norm_w"], G),
        "selcols": np.ascontiguousarray(
            np.broadcast_to(sel.reshape(1, 24), (128, 24)).astype(f)),
    }


def build_l0attn(dbg=False, dbg_stop=False, debug=False):
    nc = bass.Bass("TRN2", target_bir_lowering=False, debug=debug)
    x = nc.dram_tensor("x", [D, TOK], F32, kind="ExternalInput").ap()
    xout = nc.dram_tensor("xout", [D, TOK], F32, kind="ExternalOutput").ap()
    dbgd = None
    if dbg:
        dbgd = {"x1": nc.dram_tensor("dbg_x1", [D, TOK], F32, kind="ExternalOutput").ap(),
                "so": nc.dram_tensor("dbg_so", [D, TOK], F32, kind="ExternalOutput").ap(),
                "E": nc.dram_tensor("dbg_E", [128, G], F32, kind="ExternalOutput").ap(),
                "chE": nc.dram_tensor("dbg_chE", [128, G], F32, kind="ExternalOutput").ap(),
                "states": nc.dram_tensor("dbg_states", [128, G], F32, kind="ExternalOutput").ap(),
                "sall": nc.dram_tensor("dbg_sall", [128, NCORES, G], F32, kind="ExternalOutput").ap(),
                "n2": nc.dram_tensor("dbg_n2", [D, TOK], BF16, kind="ExternalOutput").ap(),
                "hid": nc.dram_tensor("dbg_hid", [128, 16, 1024], BF16, kind="ExternalOutput").ap()}
    with contextlib.ExitStack() as stack:
        S = Sems(nc, stack)
        B = Bufs(nc, stack)
        block = stack.enter_context(nc.Block())
        B.init_consts(block, S)
        W = declare_l0attn_weights(nc, stack, S, block, B)
        t_init = S.val("init")
        t_scales = S.val("xld")

        @block.tensor
        def _(t):
            t.wait_ge(S.get("init"), t_init)

        @block.scalar
        def _(s):
            s.wait_ge(S.get("xld"), t_scales)

        @block.vector
        def _(v):
            v.wait_ge(S.get("xld"), t_scales)

        emit_l0attn(block, nc, S, B, W, x, xout, dbg=dbgd, dbg_stop=dbg_stop)
        t_done = S.val("outd")
        t_dbg = S.val("dbgd")

        @block.sync
        def _(q):
            q.wait_ge(S.get("outd"), t_done)
            q.wait_ge(S.get("dbgd"), t_dbg)
    return nc



def emit_l1attn(block, nc, S, B, W, x_dram, xout_dram):
    PS = B.PS
    N = 64
    uv = B.uf32
    yv = B.vf32
    h0 = B.xsub_flat[:, 0:2048]
    h1 = B.xsub_flat[:, 2048:4096]
    iota16 = B.sqb_i16[:, 0:2048]
    pu4 = B.panu[:].rearrange("p a g c -> p (a g c)").bitcast(F32)
    xb4 = B.xb[:].rearrange("p g c -> p (g c)").bitcast(F32)
    sW = pu4[:, 0:512]
    sCB = pu4[:, 512:1024]
    sNW = pu4[:, 1024:1536]
    sDA = pu4[:, 1536:2048]
    xST = xb4[:, 0:512]
    xHE = xb4[:, 512:1024]
    xT = xb4[:, 1024:1536]
    xAC = xb4[:, 1536:2048]
    v8 = lambda ap: ap.rearrange("p (g n) -> p g n", g=G)
    sall = B.panh[:].rearrange("p a g c -> p (a g c)").bitcast(F32).rearrange(
        "p (j c) -> p j c", j=NCORES)

    for ft in range(2):
        emit_norm(block, nc, S, B, src_dram=x_dram, ft=ft, out=uv,
                  out_col0=ft * 1024, mean=True, eps=LN_EPS,
                  gamma=W["gam"], beta=W["bet"], double=True)
    t_xn = S.val("xnd")

    t_sums_b = S.val("sums")

    @block.gpsimd
    def _(g, t_sums_b=t_sums_b):
        g.wait_ge(S.get("sums"), t_sums_b)
        g.iota(iota16, pattern=[[1, 2048]], base=1, channel_multiplier=0).then_inc(S.get("iotd"), 1)
    t_iot = S.bump("iotd", 1)

    t_wl = S.val("l1wl")

    @block.scalar
    def _(s, t_wl=t_wl):
        s.wait_ge(S.get("l1wl"), t_wl)
        s.activation(W["dt"][:], W["ldt"][:], AF.Exp)
        s.activation(sW, sW, AF.Exp).then_inc(S.get("l1c"), 1)
    t_c1 = S.bump("l1c", 1)

    @block.vector
    def _(v, t_c1=t_c1):
        v.wait_ge(S.get("l1c"), t_c1)

        def step(instr):
            t = S.bump("vch", 1)
            instr.then_inc(S.get("vch"), 1)
            v.wait_ge(S.get("vch"), t)
        for g in range(G):
            step(v.tensor_scalar(v8(sW)[:, g, :], v8(sW)[:, g, :],
                                 W["dt"][:, g:g + 1], None, op0=ALU.mult))
        step(v.tensor_tensor(sCB, sNW, sCB, op=ALU.mult))
        for g in range(G):
            step(v.tensor_scalar(v8(sCB)[:, g, :], v8(sCB)[:, g, :],
                                 W["dt"][:, g:g + 1], None, op0=ALU.mult))
        step(v.tensor_scalar(sNW, sW, -1.0, None, op0=ALU.mult))
        v.nop().then_inc(S.get("l1c"), 1)
    t_c2 = S.bump("l1c", 1)

    @block.scalar
    def _(s, t_c2=t_c2):
        s.wait_ge(S.get("l1c"), t_c2)
        s.activation(sDA, sNW, AF.Exp)
        s.activation(xAC, sNW, AF.Exp, scale=2048.0).then_inc(S.get("l1c"), 1)
    t_c3 = S.bump("l1c", 1)

    @block.gpsimd
    def _(g, t_xn=t_xn):
        g.wait_ge(S.get("xnd"), t_xn)
        for gg in range(G):
            g.memset(yv[:, gg, :], 0.0).then_inc(S.get("yz"), 1)
    t_yz = S.bump("yz", G)

    need_out = S.val("outd")
    n_iter = N * G
    base_ssc = S.val("l1ssc")
    base_st = S.val("l1st")
    for it in range(n_iter):
        n, gg = divmod(it, G)
        hbuf = h0 if it % 2 == 0 else h1

        @block.vector
        def _(v, it=it, n=n, gg=gg, hbuf=hbuf, need_out=need_out):
            if it == 0:
                v.wait_ge(S.get("l1c"), t_c3)
                v.wait_ge(S.get("xnd"), t_xn)
                v.wait_ge(S.get("outd"), need_out)
                v.wait_ge(S.get("yz"), t_yz)
            if it >= 2:
                v.wait_ge(S.get("l1st"), base_st + it - 1)
            t_sscv = S.bump("l1ssc", 1)
            v.tensor_tensor_scan(hbuf, v8(sDA)[:, gg, n:n + 1].broadcast_to((128, TOK)),
                                 uv[:, gg, :], initial=0.0, op0=ALU.mult,
                                 op1=ALU.add).then_inc(S.get("l1ssc"), 1)
            v.wait_ge(S.get("l1ssc"), t_sscv)
            v.scalar_tensor_tensor(yv[:, gg, :], hbuf, v8(sCB)[:, gg, n:n + 1], yv[:, gg, :],
                                   op0=ALU.mult, op1=ALU.add).then_inc(S.get("l1yac"), 1)
        S.bump("l1yac", 1)

        @block.gpsimd
        def _(g, it=it, n=n, gg=gg, hbuf=hbuf):
            g.wait_ge(S.get("l1ssc"), base_ssc + it + 1)
            g.tensor_copy(v8(xST)[:, gg, n:n + 1], hbuf[:, 2047:2048]).then_inc(S.get("l1st"), 1)
        S.bump("l1st", 1)
    t_scans = S.val("l1yac")
    t_st_all = S.val("l1st")

    @block.gpsimd
    def _(g, t_scans=t_scans, t_st_all=t_st_all):
        g.wait_ge(S.get("l1yac"), t_scans)
        g.wait_ge(S.get("l1st"), t_st_all)
        g.wait_ge(S.get("l1csd"), S.val("l1csd"))
        g.dma_start(out=W["cst_in"].ap(), in_=xST).then_inc(S.get("l1csd"), 16)
        g.wait_ge(S.get("l1csd"), S.val("l1csd") + 16)
        g.collective_compute(
            "AllGather", ALU.bypass, replica_groups=[list(range(NCORES))],
            ins=[W["cst_in"].ap().opt()], outs=[W["cst_all"].ap().opt()],
        ).then_inc(S.get("l1cc"))
        g.wait_ge(S.get("l1cc"), 1)
        g.dma_start(out=sall,
                    in_=W["cst_all"].ap().rearrange("(j p) c -> p j c", p=128)
                    ).then_inc(S.get("l1csd"), 16)
    S.bump("l1csd", 32)
    S.bump("l1cc", 1)
    t_csd = S.val("l1csd")

    @block.vector
    def _(v, t_csd=t_csd):
        v.wait_ge(S.get("l1csd"), t_csd)

        def step(instr):
            t = S.bump("vch", 1)
            instr.then_inc(S.get("vch"), 1)
            v.wait_ge(S.get("vch"), t)
        for k in [2, 1, 0]:
            step(v.memset(xT, 0.0))
            for j in range(NCORES):
                step(v.scalar_tensor_tensor(xT, sall[:, j, :],
                                            W["selcols"][:, k * 8 + j:k * 8 + j + 1],
                                            xT, op0=ALU.mult, op1=ALU.add))
            if k == 2:
                step(v.tensor_copy(xHE, xT))
            else:
                step(v.tensor_tensor(xHE, xHE, xAC, op=ALU.mult))
                step(v.tensor_tensor(xHE, xHE, xT, op=ALU.add))
        v.tensor_tensor(xT, sCB, xHE, op=ALU.mult).then_inc(S.get("l1ex"), 1)
    t_ex = S.bump("l1ex", 1)
    sCBH_v = v8(xT)

    base_pw = S.val("l1pw")
    base_cr = S.val("l1cr")
    for it in range(n_iter):
        n, gg = divmod(it, G)
        pw = h0 if it % 2 == 0 else h1

        @block.scalar
        def _(s, it=it, n=n, gg=gg, pw=pw, t_ex=t_ex, t_iot=t_iot):
            if it == 0:
                s.wait_ge(S.get("l1ex"), t_ex)
                s.wait_ge(S.get("iotd"), t_iot)
            if it >= 2:
                s.wait_ge(S.get("l1cr"), base_cr + it - 1)
            s.activation(pw, iota16, AF.Exp,
                         scale=v8(sNW)[:, gg, n:n + 1]).then_inc(S.get("l1pw"), 1)
        S.bump("l1pw", 1)

        @block.vector
        def _(v, it=it, n=n, gg=gg, pw=pw, t_ex=t_ex):
            v.wait_ge(S.get("l1pw"), base_pw + it + 1)
            v.scalar_tensor_tensor(yv[:, gg, :], pw, sCBH_v[:, gg, n:n + 1],
                                   yv[:, gg, :], op0=ALU.mult, op1=ALU.add
                                   ).then_inc(S.get("l1cr"), 1)
        S.bump("l1cr", 1)
    t_cr = S.val("l1cr")

    @block.vector
    def _(v, t_cr=t_cr):
        v.wait_ge(S.get("l1cr"), t_cr)
        for gg in range(G):
            v.scalar_tensor_tensor(yv[:, gg, :], uv[:, gg, :], W["Dp"][:, gg:gg + 1],
                                   yv[:, gg, :], op0=ALU.mult, op1=ALU.add
                                   ).then_inc(S.get("l1du"), 1)
    t_du = S.bump("l1du", G)

    ybf = B.u[:].rearrange("p a c -> p (a c)")[:, 0:16384].rearrange(
        "p (g c) -> p g c", g=G)

    @block.scalar
    def _(s, t_du=t_du):
        s.wait_ge(S.get("l1du"), t_du)
        for gg in range(G):
            last = s.copy(ybf[:, gg, :], yv[:, gg, :])
        last.then_inc(S.get("l1yb"), 1)
    t_yb = S.bump("l1yb", 1)

    B.pan_last["panh"] = ("l1ex", t_ex)
    t_pw_all = S.val("l1pw")
    for ft in range(2):
        for fsub in range(2):
            c0 = ft * 1024 + fsub * FSUB
            tag = "l1o%d%d" % (ft, fsub)

            B.scr_guard = [("l1cr", t_cr), ("l1pw", t_pw_all)]
            t_bcd, t_xld = emit_norm(
                block, nc, S, B, src_dram=x_dram, ft=ft, out=None,
                mean=True, eps=LN_EPS, apply=False, subs=(fsub,),
                xsub_guards=[("l1cr", t_cr)])

            def wo(blk, og, fsub_i, bank, t_acc, tag=tag, t_xld=t_xld,
                   t_bcd=t_bcd):
                @blk.vector
                def _(v, og=og, bank=bank, t_acc=t_acc):
                    v.wait_ge(S.get("acc_" + tag), t_acc)
                    v.wait_ge(S.get("xld"), t_xld)
                    v.wait_ge(S.get("bcd"), t_bcd)
                    st0 = B.st[:, 0, :]
                    st2 = B.st[:, 1, :]
                    vstep(v, S, v.tensor_scalar(st0, B.PS[:, bank, :],
                                                W["bo"][:, og:og + 1], None,
                                                op0=ALU.add))
                    vstep(v, S, v.tensor_tensor(st0, st0, B.xsub[:, og, :],
                                                op=ALU.add))
                    vstep(v, S, v.tensor_tensor(st2, B.xsub[:, og, :],
                                                B.PS[:, 6, :], op=ALU.mult))
                    vstep(v, S, v.tensor_tensor(st2, st2, B.PS[:, 7, :],
                                                op=ALU.subtract))
                    v.tensor_tensor(B.xsub[:, og, :], st0, st2,
                                    op=ALU.add).then_inc(S.get("ev_" + tag), 1)

            emit_stage(block, nc, S, B, tag=tag, wT_dram=W["WoT"],
                       pan=B.panh, pan_nbuf=2, n_og=8, n_kg=8, rhs_tile=ybf,
                       rhs_cols0=c0, writer=wo, n_fsub=1, bank_rot=4,
                       wait_before=[("l1yb", t_yb)], pan_key="panh")
            t_res = S.val("ev_" + tag)

            @block.vector
            def _(v, t_res=t_res, tag=tag):
                v.wait_ge(S.get("ev_" + tag), t_res)
                v.nop().then_inc(S.get("xnd"), 1)
            S.bump("xnd", 1)
            B.b47_last = [("xnd", S.val("xnd"))]

            t_od = S.val("outd")

            @block.sync
            def _(q, c0=c0, t_res=t_res, tag=tag, t_od=t_od):
                q.wait_ge(S.get("ev_" + tag), t_res)
                q.wait_ge(S.get("outd"), t_od)
                q.dma_start(out=fm(xout_dram, c0, FSUB), in_=B.xsub[:]).then_inc(S.get("outd"), 16)
            S.bump("outd", 16)
    B.scr_guard = [("l1cr", t_cr), ("l1pw", S.val("l1pw"))]


def declare_l1attn_weights(nc, stack, S, block, B, prefix="m_"):
    d = {}
    d["WoT"] = nc.dram_tensor(prefix + "WoT", [1024, 1024], F32, kind="ExternalInput").ap()
    for nm, ng in [("gam", G), ("bet", G), ("ldt", G), ("Dp", G), ("bo", G),
                   ("selcols", 24)]:
        d[nm + "_dram"] = nc.dram_tensor(prefix + nm, [128, ng], F32, kind="ExternalInput").ap()
        t = stack.enter_context(nc.sbuf_tensor(prefix + nm + "_sb", [128, ng], F32))
        d[nm] = t

        tw = S.val("xld")

        @block.sync
        def _(q, t=t, src=d[nm + "_dram"], tw=tw):
            q.wait_ge(S.get("xld"), tw)
            q.dma_start(out=t[:], in_=src).then_inc(S.get("xld"), 16)
        S.bump("xld", 16)
    d["dt"] = stack.enter_context(nc.sbuf_tensor(prefix + "dt_sb", [128, G], F32))
    for nm in ["logA", "Bp", "Cp"]:
        d[nm + "_dram"] = nc.dram_tensor(prefix + nm, [1024, 64], F32, kind="ExternalInput").ap()
    d["cst_in"] = nc.dram_tensor(prefix + "cst_in", [128, 512], F32)
    d["cst_all"] = nc.dram_tensor(prefix + "cst_all", [128 * NCORES, 512], F32)
    return d


def emit_l1attn_param_loads(block, nc, S, B, W):
    pu = B.panu[:].rearrange("p a g c -> p (a g c)").bitcast(F32)
    dsts = [pu[:, 0:512], pu[:, 512:1024], pu[:, 1024:1536]]
    srcs = [W["logA_dram"], W["Bp_dram"], W["Cp_dram"]]
    pg = B.pan_last.get("panu")

    @block.gpsimd
    def _(g):
        if pg is not None:
            g.wait_ge(S.get(pg[0]), pg[1])
        base = S.val("l1wl")
        for i, (dst, src) in enumerate(zip(dsts, srcs)):
            g.wait_ge(S.get("l1wl"), base + 16 * i)
            g.dma_start(out=dst.rearrange("p (g n) -> p g n", g=G),
                        in_=src.rearrange("(g p) n -> p g n", p=128)
                        ).then_inc(S.get("l1wl"), 16)
    S.bump("l1wl", 48)


def l1attn_host_weights(inp, core):
    f = np.float32
    b, qc = core // 4, core % 4
    sel = np.zeros((3, 8), f)
    for k in range(3):
        j = qc - 1 - k
        if j >= 0:
            sel[k, b * 4 + j] = 1.0
    return {
        "WoT": np.ascontiguousarray(np.asarray(inp["L1_attn_W_out"], f).T),
        "gam": pack_scale(inp["L1_attn_ln_gamma"], G),
        "bet": pack_scale(inp["L1_attn_ln_beta"], G),
        "ldt": pack_scale(inp["L1_attn_log_dt"], G),
        "Dp": pack_scale(inp["L1_attn_D_param"], G),
        "bo": pack_scale(inp["L1_attn_b_out"], G),
        "selcols": np.ascontiguousarray(
            np.broadcast_to(sel.reshape(1, 24), (128, 24)).astype(f)),
        "logA": np.ascontiguousarray(np.asarray(inp["L1_attn_log_A"], f)),
        "Bp": np.ascontiguousarray(np.asarray(inp["L1_attn_B_param"], f)),
        "Cp": np.ascontiguousarray(np.asarray(inp["L1_attn_C_param"], f)),
    }


def build_l1attn(debug=False):
    nc = bass.Bass("TRN2", target_bir_lowering=False, debug=debug)
    x = nc.dram_tensor("x", [D, TOK], F32, kind="ExternalInput").ap()
    xout = nc.dram_tensor("xout", [D, TOK], F32, kind="ExternalOutput").ap()
    with contextlib.ExitStack() as stack:
        S = Sems(nc, stack)
        B = Bufs(nc, stack)
        block = stack.enter_context(nc.Block())
        B.init_consts(block, S)
        W = declare_l1attn_weights(nc, stack, S, block, B)
        emit_l1attn_param_loads(block, nc, S, B, W)
        t_init = S.val("init")
        t_scales = S.val("xld")

        @block.tensor
        def _(t):
            t.wait_ge(S.get("init"), t_init)

        @block.scalar
        def _(s):
            s.wait_ge(S.get("xld"), t_scales)

        @block.vector
        def _(v):
            v.wait_ge(S.get("xld"), t_scales)

        @block.gpsimd
        def _(g):
            g.wait_ge(S.get("xld"), t_scales)

        emit_l1attn(block, nc, S, B, W, x, xout)
        t_done = S.val("outd")

        @block.sync
        def _(q):
            q.wait_ge(S.get("outd"), t_done)
    return nc



MOE_CAP = 2926
MOE_CAPC = 384
MOE_E = 7
MOE_CH = MOE_CAPC // 128
MOE_ROWS = MOE_CAPC * MOE_E * NCORES
MOE_SH = MOE_CAPC * MOE_E
MOE_NCH = MOE_SH // 128
MOE_ZW = MOE_ROWS // 128
MOE_FS = [(0, MOE_CAPC)]


def vstep(v, S, instr):
    t = S.bump("vch", 1)
    instr.then_inc(S.get("vch"), 1)
    v.wait_ge(S.get("vch"), t)


def emit_pair_stage(block, nc, S, B, *, tag, w1_dram, w2_dram, rhs, rhs_cols0,
                    fsubs, n_og, n_kg, dst, silu_scr, wait_first=(),
                    pan_guard=None):
    PS = B.PS
    accs = S.get("acc_" + tag)
    evs = S.get("ev_" + tag)
    sil = S.get("sil_" + tag)
    base_acc = S.val("acc_" + tag)
    base_ev = S.val("ev_" + tag)
    base_sil = S.val("sil_" + tag)
    wsl = [[S.get("w_%s_%d%d" % (tag, s, wi)) for wi in range(2)] for s in range(2)]
    base_wsl = [[S.val("w_%s_%d%d" % (tag, s, wi)) for wi in range(2)] for s in range(2)]
    nf = len(fsubs)

    def load_pair_m(g, j):
        for wi, wT in enumerate((w1_dram, w2_dram)):
            srcw = wT[:, j * 128:(j + 1) * 128].rearrange("(g p) c -> p g c", p=128)
            g.wait_ge(wsl[j % 2][wi], S.val("w_%s_%d%d" % (tag, j % 2, wi)))
            g.dma_start(out=B.panu[:, (j % 2) * 2 + wi, 0:n_kg, :],
                        in_=srcw).then_inc(wsl[j % 2][wi], 16)
            S.bump("w_%s_%d%d" % (tag, j % 2, wi), 16)

    for og in range(n_og):
        @block.gpsimd
        def _(g, og=og):
            if og == 0:
                if pan_guard is not None:
                    g.wait_ge(S.get(pan_guard[0]), pan_guard[1])
                for j in range(min(2, n_og)):
                    load_pair_m(g, j)
            else:
                j = og + 1
                if j < n_og:
                    g.wait_ge(accs, base_acc + (j - 1) * 2 * nf)
                    load_pair_m(g, j)

        @block.tensor
        def _(t, og=og):
            for wi in range(2):
                t.wait_ge(wsl[og % 2][wi],
                          base_wsl[og % 2][wi] + 16 * (og // 2 + 1))
            if og == 0:
                for nm, vv in wait_first:
                    t.wait_ge(S.get(nm), vv)
            for fi, (fo, fw) in enumerate(fsubs):
                ba = (og % 2) * 2 * nf + fi * 2
                bb = ba + 1
                if og >= 2:
                    t.wait_ge(evs, base_ev + (og - 2) * nf + fi + 1)
                cs = slice(rhs_cols0 + fo, rhs_cols0 + fo + fw)
                for k in range(n_kg):
                    mm = t.matmul(PS[:, ba, 0:fw], B.panu[:, (og % 2) * 2, k, :],
                                  rhs[:, k, cs], start=(k == 0), stop=(k == n_kg - 1))
                mm.then_inc(accs, 1)
                for k in range(n_kg):
                    mm = t.matmul(PS[:, bb, 0:fw], B.panu[:, (og % 2) * 2 + 1, k, :],
                                  rhs[:, k, cs], start=(k == 0), stop=(k == n_kg - 1))
                mm.then_inc(accs, 1)

        for fi, (fo, fw) in enumerate(fsubs):
            t_acc = S.bump("acc_" + tag, 2)
            ba = (og % 2) * 2 * nf + fi * 2
            bb = ba + 1
            scr = silu_scr[fi]
            need_ev = base_ev + (og - 1) * nf + fi + 1 if og >= 1 else base_ev

            @block.scalar
            def _(s, og=og, fi=fi, fw=fw, ba=ba, t_acc=t_acc, need_ev=need_ev, scr=scr):
                s.wait_ge(evs, need_ev)
                s.wait_ge(accs, t_acc - 1)
                s.activation(scr[:, 0:fw], PS[:, ba, 0:fw], AF.Silu).then_inc(sil, 1)
            t_sil = S.bump("sil_" + tag, 1)

            @block.vector
            def _(v, og=og, fi=fi, fo=fo, fw=fw, bb=bb, t_acc=t_acc, t_sil=t_sil, scr=scr):
                v.wait_ge(accs, t_acc)
                v.wait_ge(sil, t_sil)
                v.tensor_tensor(dst[:, og, fo:fo + fw], PS[:, bb, 0:fw],
                                scr[:, 0:fw], op=ALU.mult).then_inc(evs, 1)
            S.bump("ev_" + tag, 1)


def emit_single_stage(block, nc, S, B, *, tag, w_dram, rhs, rhs_cols0, fsubs,
                      n_og, n_kg, writer, pan, pan_nbuf, wait_first=(),
                      pan_guard=None, bank_rot=2):
    PS = B.PS
    accs = S.get("acc_" + tag)
    evs = S.get("ev_" + tag)
    base_acc = S.val("acc_" + tag)
    base_ev = S.val("ev_" + tag)
    wsl = [S.get("w_%s_%d" % (tag, s)) for s in range(pan_nbuf)]
    base_wsl = [S.val("w_%s_%d" % (tag, s)) for s in range(pan_nbuf)]
    nf = len(fsubs)

    def load_panel(g, j):
        src = w_dram[:, j * 128:(j + 1) * 128].rearrange("(g p) c -> p g c", p=128)
        g.wait_ge(wsl[j % pan_nbuf], S.val("w_%s_%d" % (tag, j % pan_nbuf)) +
                  16 * 0)
        g.dma_start(out=pan[:, j % pan_nbuf, 0:n_kg, :], in_=src
                    ).then_inc(wsl[j % pan_nbuf], 16)

    for og in range(n_og):
        @block.gpsimd
        def _(g, og=og):
            if og == 0:
                if pan_guard is not None:
                    g.wait_ge(S.get(pan_guard[0]), pan_guard[1])
                for j in range(min(pan_nbuf, n_og)):
                    load_panel(g, j)
            else:
                j = og + pan_nbuf - 1
                if j < n_og:
                    g.wait_ge(accs, base_acc + (j - pan_nbuf + 1) * nf)
                    load_panel(g, j)
        if og == 0:
            for j in range(min(pan_nbuf, n_og)):
                S.bump("w_%s_%d" % (tag, j % pan_nbuf), 16)
        else:
            j = og + pan_nbuf - 1
            if j < n_og:
                S.bump("w_%s_%d" % (tag, j % pan_nbuf), 16)

        @block.tensor
        def _(t, og=og):
            t.wait_ge(wsl[og % pan_nbuf],
                      base_wsl[og % pan_nbuf] + 16 * (og // pan_nbuf + 1))
            if og == 0:
                for nm, vv in wait_first:
                    t.wait_ge(S.get(nm), vv)
            for fi, (fo, fw) in enumerate(fsubs):
                bank = (og % bank_rot) * nf + fi
                if og >= bank_rot:
                    t.wait_ge(evs, base_ev + (og - bank_rot) * nf + fi + 1)
                cs = slice(rhs_cols0 + fo, rhs_cols0 + fo + fw)
                for k in range(n_kg):
                    mm = t.matmul(PS[:, bank, 0:fw], pan[:, og % pan_nbuf, k, :],
                                  rhs[:, k, cs], start=(k == 0), stop=(k == n_kg - 1))
                mm.then_inc(accs, 1)
        for fi, (fo, fw) in enumerate(fsubs):
            t_acc = S.bump("acc_" + tag, 1)
            bank = (og % bank_rot) * nf + fi
            writer(block, og, fi, fo, fw, bank, t_acc)
            S.bump("ev_" + tag, 1)


def emit_moe(block, nc, S, B, W, x_dram, xout_dram, uid):
    PS = B.PS
    hv = B.uf32
    uflat = B.u[:].rearrange("p a c -> p (a c)")
    uf32 = uflat.bitcast(F32)
    vflat = B.v[:].rearrange("p a c -> p (a c)")
    hbf = vflat[:, 0:16384].rearrange("p (g c) -> p g c", g=G)
    tmrow = B.sqb[:, 0:2, :].rearrange("p a c -> p (a c)")
    tmrow2 = B.sqb[:, 2:4, :].rearrange("p a c -> p (a c)")
    xbf32 = B.xb[:].rearrange("p g c -> p (g c)").bitcast(F32)
    itab = xbf32[:, 0:32].rearrange("p (a k) -> p a k", k=2)
    dtab = xbf32[:, 32:48]
    gtab = xbf32[:, 48:80].rearrange("p (a k) -> p a k", k=2)
    ftab = xbf32[:, 80:112].rearrange("p (a k) -> p a k", k=2)
    ktab = xbf32[:, 112:144].rearrange("p (a k) -> p a k", k=2)
    sscrA = xbf32[:, 512:1024]
    zsc = xbf32[:, 1024:1024 + MOE_ZW]
    sscrB = xbf32[:, 1536:2048]
    sqf32 = B.sqb[:].rearrange("p g c -> p (g c)").bitcast(F32)
    rowA = sqf32[0:1, :]
    vemf = vflat.bitcast(F32)
    EA = uf32[:, 8192:10240][0:7, :]
    EB = uf32[:, 10240:12288][0:7, :]
    EC = uf32[:, 12288:14336][0:7, :]
    ED = uf32[:, 14336:16384][0:7, :]
    EE = vemf[:, 8192:10240][0:7, :]
    prod7 = None
    hs = uflat[:, 0:16384].rearrange("p (a c) -> p a c", c=512)
    yebf = uflat[:, 24576:24576 + G * MOE_CAPC].rearrange("p (g c) -> p g c", g=G)
    xef = vflat[:, 0:G * MOE_CAPC].rearrange("p (g c) -> p g c", g=G)
    hev = vflat[:, G * MOE_CAPC:(G + 32) * MOE_CAPC].rearrange(
        "p (a c) -> p a c", c=MOE_CAPC)
    routed = B.vf32
    rt32 = B.xsub_flat[:, 0:1024]
    exscr = [B.xsub_flat[:, 1024:1536], B.xsub_flat[:, 1536:2048]]
    U = W["U"]
    rsc = B.st[:, 1, :]

    if not hasattr(B, "rb_map"):
        @block.gpsimd
        def _(g):
            B.rb_map = g.alloc_register("rb_map")
            B.rb_h = g.alloc_register("rb_h")
            g.reg_mov(B.rb_map, MOE_ROWS - 1)
            g.reg_mov(B.rb_h, TOK * NCORES - 1)

    for ft in range(2):
        emit_norm(block, nc, S, B, src_dram=x_dram, ft=ft, out=hv,
                  out_col0=ft * 1024, mean=True, eps=LN_EPS,
                  gamma=W["lng"], beta=W["lnb"], double=True)
    t_h = S.val("xnd")

    @block.scalar
    def _(s, t_h=t_h):
        s.wait_ge(S.get("xnd"), t_h)
        for g in range(G):
            last = s.copy(hbf[:, g, :], hv[:, g, :])
        last.then_inc(S.get("hbd"), 1)
    t_hb = S.bump("hbd", 1)

    psbf = lambda b: PS[:, b, :].bitcast(BF16)
    b03 = list(B.b03_last)
    B.b03_last = []
    base_tp = S.val("tpd")
    base_te = S.val("tpe")
    base_hs = S.val("hsd")
    for ch in range(16):
        @block.tensor
        def _(t, ch=ch, t_hb=t_hb, b03=tuple(b03)):
            if ch == 0:
                t.wait_ge(S.get("hbd"), t_hb)
                for nm, vv in b03:
                    t.wait_ge(S.get(nm), vv)
            if ch >= 2:
                t.wait_ge(S.get("tpe"), base_te + ch - 1)
            for g in range(G):
                mm = t.transpose(psbf(ch % 2)[:, g * 128:(g + 1) * 128],
                                 hbf[:, g, ch * 128:(ch + 1) * 128], W["identbf"][:])
            mm.then_inc(S.get("tpd"), 1)
        S.bump("tpd", 1)

        @block.vector
        def _(v, ch=ch):
            v.wait_ge(S.get("tpd"), base_tp + ch + 1)
            if ch >= 1:
                v.wait_ge(S.get("hsd"), base_hs + 16 * ch)
            v.tensor_copy(tmrow, psbf(ch % 2)[:, 0:1024]).then_inc(S.get("tpe"), 1)
        S.bump("tpe", 1)

        t_hsw = S.val("hsd")

        @block.sync
        def _(q, ch=ch, t_hsw=t_hsw):
            q.wait_ge(S.get("tpe"), base_te + ch + 1)
            q.wait_ge(S.get("hsd"), t_hsw)
            q.dma_start(out=U["hshard"].ap()[ch * 128:(ch + 1) * 128, :],
                        in_=tmrow).then_inc(S.get("hsd"), 16)
        S.bump("hsd", 16)
    t_hs = S.val("hsd")

    o1s = rsc[:, 0:7]
    t7a = rsc[:, 16:23]
    t7b = rsc[:, 32:39]
    l2s = rsc[:, 48:55]
    m1s = rsc[:, 64:65]
    m2s = rsc[:, 80:81]
    s1s = rsc[:, 96:97]
    base_rl = S.val("rl")
    base_rv = S.val("rv")
    t_rw = S.val("rwl")
    for ch in range(16):
        @block.tensor
        def _(t, ch=ch, t_h=t_h, t_rw=t_rw):
            if ch == 0:
                t.wait_ge(S.get("xnd"), t_h)
                t.wait_ge(S.get("rwl"), t_rw)
            if ch >= 2:
                t.wait_ge(S.get("rv"), base_rv + ch - 1)
            bank = 2 + ch % 2
            for g in range(G):
                mm = t.matmul(PS[:, bank, 0:7], hv[:, g, ch * 128:(ch + 1) * 128],
                              W["wr"][:, g, :], start=(g == 0), stop=(g == G - 1))
            mm.then_inc(S.get("rl"), 1)
        S.bump("rl", 1)

        @block.vector
        def _(v, ch=ch):
            v.wait_ge(S.get("rl"), base_rl + ch + 1)
            if ch >= 1:
                v.wait_ge(S.get("rv"), base_rv + ch)
            l = PS[:, 2 + ch % 2, 0:7]
            vstep(v, S, v.tensor_reduce(m1s, l, axis=AX.X, op=ALU.max))
            vstep(v, S, v.tensor_scalar(t7a, l, m1s, None, op0=ALU.is_equal))
            vstep(v, S, v.tensor_tensor(t7b, t7a, W["mdesc"][:, 0:7], op=ALU.mult))
            vstep(v, S, v.tensor_reduce(s1s, t7b, axis=AX.X, op=ALU.max))
            vstep(v, S, v.tensor_scalar(itab[:, ch, 0:1], s1s, -1.0, 7.0,
                                        op0=ALU.mult, op1=ALU.add))
            vstep(v, S, v.tensor_scalar(o1s, W["miota"][:, 0:7], itab[:, ch, 0:1],
                                        None, op0=ALU.is_equal))
            vstep(v, S, v.tensor_scalar(t7a, o1s, -1e30, None, op0=ALU.mult))
            vstep(v, S, v.tensor_tensor(l2s, l, t7a, op=ALU.add))
            vstep(v, S, v.tensor_reduce(m2s, l2s, axis=AX.X, op=ALU.max))
            vstep(v, S, v.tensor_scalar(t7a, l2s, m2s, None, op0=ALU.is_equal))
            vstep(v, S, v.tensor_tensor(t7b, t7a, W["mdesc"][:, 0:7], op=ALU.mult))
            vstep(v, S, v.tensor_reduce(s1s, t7b, axis=AX.X, op=ALU.max))
            vstep(v, S, v.tensor_scalar(itab[:, ch, 1:2], s1s, -1.0, 7.0,
                                        op0=ALU.mult, op1=ALU.add))
            v.tensor_tensor(dtab[:, ch:ch + 1], m1s, m2s,
                            op=ALU.subtract).then_inc(S.get("rv"), 1)
        S.bump("rv", 1)
    t_rv = S.val("rv")

    def build_idx_row(w_i, first_guards=()):
        base_ir = S.val("irp")
        base_ie = S.val("ire")
        for ch in range(16):
            @block.tensor
            def _(t, ch=ch, w_i=w_i, fg=tuple(first_guards), base_ie=base_ie):
                if ch == 0:
                    for nm, vv in fg:
                        t.wait_ge(S.get(nm), vv)
                if ch >= 1:
                    t.wait_ge(S.get("ire"), base_ie + ch)
                t.transpose(PS[0:1, 4, (ch % 4) * 128:(ch % 4) * 128 + 128],
                            itab[:, ch, w_i:w_i + 1], W["identf"][:]).then_inc(S.get("irp"), 1)
            S.bump("irp", 1)

            @block.vector
            def _(v, ch=ch, base_ir=base_ir):
                v.wait_ge(S.get("irp"), base_ir + ch + 1)
                v.tensor_copy(rowA[0:1, ch * 128:(ch + 1) * 128],
                              PS[0:1, 4, (ch % 4) * 128:(ch % 4) * 128 + 128]
                              ).then_inc(S.get("ire"), 1)
            S.bump("ire", 1)
        return S.val("ire")

    def build_O(t_row):
        base_ob = S.val("obd")
        for cs in range(4):
            @block.tensor
            def _(t, cs=cs, t_row=t_row, base_ob=base_ob):
                if cs == 0:
                    t.wait_ge(S.get("ire"), t_row)
                if cs >= 2:
                    t.wait_ge(S.get("obd"), base_ob + cs - 1)
                t.matmul(PS[0:7, 5 + cs % 2, :], W["onesrow"][0:1, 0:7],
                         rowA[0:1, cs * 512:(cs + 1) * 512],
                         start=True, stop=True).then_inc(S.get("obp"), 1)
            t_obp = S.bump("obp", 1)

            @block.vector
            def _(v, cs=cs, t_obp=t_obp):
                v.wait_ge(S.get("obp"), t_obp)
                v.tensor_scalar(ED[:, cs * 512:(cs + 1) * 512], PS[0:7, 5 + cs % 2, :],
                                W["mcols"][0:7, 0:1], None,
                                op0=ALU.is_equal).then_inc(S.get("obd"), 1)
            S.bump("obd", 1)
        return S.val("obd")

    need_statb = S.val("statb")
    t_r1 = build_idx_row(0, [("rv", t_rv), ("statb", need_statb), ("hsd", t_hs)])
    t_o1 = build_O(t_r1)

    @block.vector
    def _(v, t_o1=t_o1):
        v.wait_ge(S.get("obd"), t_o1)
        ones7 = W["onescol"][0:7, 0:1].broadcast_to((7, 2048))
        vstep(v, S, v.tensor_tensor_scan(EB, ones7, ED, initial=0.0,
                                         op0=ALU.mult, op1=ALU.add))
        v.tensor_copy(W["cnt"][0:7, 0:1], EB[:, 2047:2048]).then_inc(S.get("cntd"), 1)
    S.bump("cntd", 1)
    t_r2 = build_idx_row(1, [("cntd", S.val("cntd"))])
    t_o2 = build_O(t_r2)

    @block.vector
    def _(v, t_o2=t_o2):
        v.wait_ge(S.get("obd"), t_o2)
        ones7 = W["onescol"][0:7, 0:1].broadcast_to((7, 2048))
        vstep(v, S, v.tensor_tensor_scan(EC, ones7, ED, initial=0.0,
                                         op0=ALU.mult, op1=ALU.add))
        v.tensor_copy(W["cnt"][0:7, 1:2], EC[:, 2047:2048]).then_inc(S.get("cntd"), 1)
    t_cnt = S.bump("cntd", 1)

    @block.gpsimd
    def _(g, t_hs=t_hs, t_cnt=t_cnt):
        g.wait_ge(S.get("hsd"), t_hs)
        g.collective_compute(
            "AllGather", ALU.bypass, replica_groups=[list(range(NCORES))],
            ins=[U["hshard"].ap().opt()], outs=[U["hfull"].ap().opt()],
        ).then_inc(S.get("cch" + uid))
        g.wait_ge(S.get("cntd"), t_cnt)
        g.wait_ge(S.get("mdma"), S.val("mdma"))
        g.dma_start(out=U["cntin"].ap(), in_=W["cnt"][0:7, :]).then_inc(S.get("mdma"), 16)
        g.wait_ge(S.get("mdma"), S.val("mdma") + 16)
        g.collective_compute(
            "AllGather", ALU.bypass, replica_groups=[list(range(NCORES))],
            ins=[U["cntin"].ap().opt()], outs=[U["cntall"].ap().opt()],
        ).then_inc(S.get("ccc" + uid))
        g.wait_ge(S.get("ccc" + uid), 1)
        g.dma_start(out=W["cnta"][0:8, 0:14],
                    in_=U["cntall"].ap().rearrange("(j e) k -> j (e k)", j=NCORES)
                    ).then_inc(S.get("mdma"), 16)
    S.bump("mdma", 32)
    S.bump("cch" + uid, 1)
    S.bump("ccc" + uid, 1)
    t_cnta = S.val("mdma")

    @block.tensor
    def _(t, t_cnta=t_cnta):
        t.wait_ge(S.get("mdma"), t_cnta)
        t.wait_ge(S.get("ire"), S.val("ire"))
        t.matmul(PS[0:1, 4, 0:14], W["mmask"][0:8, 0:1], W["cnta"][0:8, 0:14],
                 start=True, stop=True)
        t.matmul(PS[0:1, 5, 0:14], W["mmask"][0:8, 1:2], W["cnta"][0:8, 0:14],
                 start=True, stop=True).then_inc(S.get("bsp"), 1)
    t_bsp = S.bump("bsp", 1)

    @block.vector
    def _(v, t_bsp=t_bsp):
        v.wait_ge(S.get("bsp"), t_bsp)
        plt = PS[0:1, 4, 0:14].rearrange("p (e k) -> p e k", k=2)
        pal = PS[0:1, 5, 0:14].rearrange("p (e k) -> p e k", k=2)
        vstep(v, S, v.tensor_scalar(W["brow"][0:1, 0:7], plt[0:1, :, 0], -1.0,
                                    None, op0=ALU.add))
        vstep(v, S, v.tensor_copy(W["brow"][0:1, 16:23], pal[0:1, :, 0]))
        vstep(v, S, v.tensor_tensor(W["brow"][0:1, 16:23], W["brow"][0:1, 16:23],
                                    plt[0:1, :, 1], op=ALU.add))
        v.tensor_scalar(W["brow"][0:1, 16:23], W["brow"][0:1, 16:23], -1.0, None,
                        op0=ALU.add).then_inc(S.get("bsd"), 1)
    t_bsd = S.bump("bsd", 1)

    @block.tensor
    def _(t, t_bsd=t_bsd):
        t.wait_ge(S.get("bsd"), t_bsd)
        t.transpose(PS[0:7, 5, 0:1], W["brow"][0:1, 0:7], W["identf"][0:1, 0:1])
        t.transpose(PS[0:7, 5, 128:129], W["brow"][0:1, 16:23],
                    W["identf"][0:1, 0:1]).then_inc(S.get("bsp"), 1)
    t_bsp2 = S.bump("bsp", 1)

    @block.vector
    def _(v, t_bsp2=t_bsp2):
        v.wait_ge(S.get("bsp"), t_bsp2)
        vstep(v, S, v.tensor_copy(W["bcol"][0:7, 0:1], PS[0:7, 5, 0:1]))
        v.tensor_copy(W["bcol"][0:7, 1:2], PS[0:7, 5, 128:129]).then_inc(S.get("bsd"), 1)
    t_bsd2 = S.bump("bsd", 1)

    prod7 = rsc[0:7, 0:512]
    for w_i, CE in enumerate([EB, EC]):
        t_prev = S.bump("posd", 0)

        @block.vector
        def _(v, CE=CE, w_i=w_i, t_bsd2=t_bsd2):
            v.wait_ge(S.get("bsd"), t_bsd2)
            if w_i == 1:
                v.wait_ge(S.get("trd"), S.val("trd"))
            vstep(v, S, v.tensor_scalar(CE, CE, W["bcol"][0:7, w_i:w_i + 1],
                                        None, op0=ALU.add))
            vstep(v, S, v.tensor_scalar(EA, CE, float(MOE_CAP), None,
                                        op0=ALU.is_lt))
            vstep(v, S, v.tensor_scalar(EE, CE, float(MOE_CAPC), None,
                                        op0=ALU.is_ge))
            for k in range(2, 8):
                vstep(v, S, v.tensor_scalar(ED, CE, float(MOE_CAPC) * k, None,
                                            op0=ALU.is_ge))
                vstep(v, S, v.tensor_tensor(EE, EE, ED, op=ALU.add))
            vstep(v, S, v.tensor_scalar(EE, EE, float(MOE_SH - MOE_CAPC), None,
                                        op0=ALU.mult))
            vstep(v, S, v.tensor_tensor(EE, EE, CE, op=ALU.add))
            vstep(v, S, v.tensor_scalar(EE, EE, W["mcols"][0:7, 1:2], None,
                                        op0=ALU.add))
            vstep(v, S, v.tensor_tensor(CE, EE, EA, op=ALU.mult))
            vstep(v, S, v.tensor_scalar(EE, EA, -4.0e9, 4.0e9,
                                        op0=ALU.mult, op1=ALU.add))
            vstep(v, S, v.tensor_tensor(CE, CE, EE, op=ALU.add))
            v.nop().then_inc(S.get("posd"), 1)
        t_pos = S.bump("posd", 1)

        t_rw2 = build_idx_row(w_i, [("posd", t_pos)])
        t_ow = build_O(t_rw2)

        for r_i, (srcE, dtab2) in enumerate([(CE, ftab), (EA, ktab)]):
            base_exp = S.val("exp")
            base_exd = S.val("exd")
            for cs in range(4):
                need_exp = S.val("exp")

                @block.vector
                def _(v, cs=cs, srcE=srcE, t_ow=t_ow, need_exp=need_exp):
                    if cs == 0:
                        v.wait_ge(S.get("obd"), t_ow)
                    v.wait_ge(S.get("exp"), need_exp)
                    v.tensor_tensor(prod7, ED[:, cs * 512:(cs + 1) * 512],
                                    srcE[:, cs * 512:(cs + 1) * 512],
                                    op=ALU.mult).then_inc(S.get("exm"), 1)
                t_exm = S.bump("exm", 1)

                @block.tensor
                def _(t, cs=cs, t_exm=t_exm, base_exd=base_exd):
                    t.wait_ge(S.get("exm"), t_exm)
                    if cs >= 2:
                        t.wait_ge(S.get("exd"), base_exd + cs - 1)
                    t.matmul(PS[0:1, 6 + cs % 2, 0:512], W["onescol"][0:7, 0:1],
                             prod7, start=True, stop=True).then_inc(S.get("exp"), 1)
                t_exp = S.bump("exp", 1)

                @block.vector
                def _(v, cs=cs, t_exp=t_exp):
                    v.wait_ge(S.get("exp"), t_exp)
                    v.tensor_copy(rowA[0:1, cs * 512:(cs + 1) * 512],
                                  PS[0:1, 6 + cs % 2, 0:512]).then_inc(S.get("exd"), 1)
                S.bump("exd", 1)
            t_rows = S.val("exd")

            base_tr = S.val("trp")
            base_td = S.val("trd")
            for ch in range(16):
                @block.tensor
                def _(t, ch=ch, t_rows=t_rows, base_td=base_td):
                    if ch == 0:
                        t.wait_ge(S.get("exd"), t_rows)
                    if ch >= 1:
                        t.wait_ge(S.get("trd"), base_td + ch)
                    t.transpose(PS[:, 4, (ch % 4):(ch % 4) + 1],
                                rowA[0:1, ch * 128:(ch + 1) * 128],
                                W["identf"][0:1, 0:1]).then_inc(S.get("trp"), 1)
                S.bump("trp", 1)

                @block.vector
                def _(v, ch=ch, dtab2=dtab2, w_i=w_i, base_tr=base_tr):
                    v.wait_ge(S.get("trp"), base_tr + ch + 1)
                    v.tensor_copy(dtab2[:, ch, w_i:w_i + 1],
                                  PS[:, 4, (ch % 4):(ch % 4) + 1]).then_inc(S.get("trd"), 1)
                S.bump("trd", 1)
    t_ext = S.val("trd")

    @block.scalar
    def _(s, t_ext=t_ext):
        s.wait_ge(S.get("trd"), t_ext)
        s.activation(gtab[:, :, 0], dtab, AF.Sigmoid)
        s.activation(gtab[:, :, 1], dtab, AF.Sigmoid, scale=-1.0).then_inc(S.get("gsd"), 1)
    t_gs = S.bump("gsd", 1)

    @block.vector
    def _(v, t_gs=t_gs):
        v.wait_ge(S.get("gsd"), t_gs)
        vstep(v, S, v.tensor_tensor(gtab[:, :, 0], gtab[:, :, 0], ktab[:, :, 0],
                                    op=ALU.mult))
        vstep(v, S, v.tensor_tensor(gtab[:, :, 1], gtab[:, :, 1], ktab[:, :, 1],
                                    op=ALU.mult))
        vstep(v, S, v.tensor_copy(W["fu32"][:], ftab.rearrange("p a k -> p (a k)")))
        v.memset(zsc, 0.0).then_inc(S.get("gmd"), 1)
    t_gm = S.bump("gmd", 1)

    @block.gpsimd
    def _(g, t_gm=t_gm):
        g.wait_ge(S.get("gmd"), t_gm)
        g.wait_ge(S.get("mdma"), S.val("mdma"))
        g.dma_start(out=U["map"].ap().rearrange("(p a) k -> p (a k)", p=128),
                    in_=zsc).then_inc(S.get("mdma"), 16)
        g.wait_ge(S.get("mdma"), S.val("mdma") + 16)
        base_msc = S.val("msc")
        for j in range(32):
            ch, k = divmod(j, 2)
            g.wait_ge(S.get("msc"), base_msc + 16 * j)
            g.indirect_dma_start(
                out=U["map"].ap(),
                out_offset=IndirectOffsetOnAxis(
                    ap=W["fu32"][:, ch * 2 + k:ch * 2 + k + 1], axis=0),
                in_=W["tokb"][:, ch:ch + 1],
                in_offset=None, bounds_check=B.rb_map, oob_is_err=False,
            ).then_inc(S.get("msc"), 16)
        g.wait_ge(S.get("msc"), base_msc + 32 * 16)
        g.collective_compute(
            "ReduceScatter", ALU.add, replica_groups=[list(range(NCORES))],
            ins=[U["map"].ap().opt()], outs=[U["mapown"].ap().opt()],
        ).then_inc(S.get("ccm" + uid))
        g.wait_ge(S.get("ccm" + uid), 1)
        g.wait_ge(S.get("mdma"), S.val("mdma") + 16)
        g.dma_start(out=W["mapc"][0:MOE_NCH, :],
                    in_=U["mapown"].ap().rearrange("(a p) k -> a (p k)", p=128)
                    ).then_inc(S.get("mdma"), 16)
    S.bump("mdma", 32)
    S.bump("msc", 32 * 16)
    S.bump("ccm" + uid, 1)
    t_mapc = S.val("mdma")

    @block.tensor
    def _(t, t_mapc=t_mapc, t_ext=t_ext):
        t.wait_ge(S.get("mdma"), t_mapc)
        t.wait_ge(S.get("trd"), t_ext)
        t.transpose(PS[:, 4, 0:MOE_NCH], W["mapc"][0:MOE_NCH, :],
                    W["identf"][0:MOE_NCH, 0:MOE_NCH]).then_inc(S.get("mcp"), 1)
    t_mcp = S.bump("mcp", 1)

    @block.vector
    def _(v, t_mcp=t_mcp):
        v.wait_ge(S.get("mcp"), t_mcp)
        vstep(v, S, v.tensor_scalar(W["mapcf"][:], PS[:, 4, 0:MOE_NCH], -1.0,
                                    None, op0=ALU.add))
        v.tensor_copy(W["mapu"][:], W["mapcf"][:]).then_inc(S.get("mcd"), 1)
    t_mcd = S.bump("mcd", 1)

    for tt in range(4):
        cols0 = tt * FSUB
        pan_guard = B.pan_last.get("panu") if tt == 0 else None
        emit_pair_stage(block, nc, S, B, tag="ms", w1_dram=W["W1s"],
                        w2_dram=W["W3s"], rhs=hbf, rhs_cols0=cols0,
                        fsubs=[(0, 512)], n_og=32, n_kg=G, dst=hs,
                        silu_scr=[sscrA],
                        wait_first=([("hbd", t_hb), ("ev_mo", S.val("ev_mo"))] +
                                    ([("rv", t_rv), ("tpe", S.val("tpe"))]
                                     if tt == 0 else [])),
                        pan_guard=pan_guard)
        t_hs_done = S.val("ev_ms")

        need_shd = S.val("shd")

        def wsh(blk, og, fi, fo, fw, bank, t_acc2, need_shd=need_shd):
            @blk.vector
            def _(v, og=og, bank=bank, t_acc2=t_acc2, fw=fw):
                v.wait_ge(S.get("acc_mo"), t_acc2)
                v.wait_ge(S.get("shd"), need_shd)
                v.tensor_copy(B.xsub[:, og, 0:fw], B.PS[:, bank, 0:fw]
                              ).then_inc(S.get("ev_mo"), 1)

        emit_single_stage(block, nc, S, B, tag="mo", w_dram=W["W2s"],
                          rhs=hs, rhs_cols0=0, fsubs=[(0, 512)], n_og=8,
                          n_kg=32, writer=wsh, pan=B.panh, pan_nbuf=2,
                          wait_first=[("ev_ms", t_hs_done)],
                          pan_guard=B.pan_last.get("panh") if tt == 0 else None,
                          bank_rot=4)
        t_mo = S.val("ev_mo")

        t_shw = S.val("shd")

        @block.sync
        def _(q, tt=tt, t_mo=t_mo, t_shw=t_shw):
            q.wait_ge(S.get("ev_mo"), t_mo)
            q.wait_ge(S.get("shd"), t_shw)
            q.dma_start(out=fm(U["shdram"].ap(), tt * FSUB, FSUB),
                        in_=B.xsub[:]).then_inc(S.get("shd"), 16)
        S.bump("shd", 16)
    t_sh = S.val("shd")
    t_ms_all = S.val("acc_ms")
    t_mo_all = S.val("ev_mo")

    for e in range(MOE_E):
        base_xt = S.val("xtp")
        base_xe = S.val("xte")
        for ch in range(MOE_CH):
            need_xtp = S.val("xtp")

            @block.gpsimd
            def _(g, e=e, ch=ch, need_xtp=need_xtp, t_mcd=t_mcd):
                if e == 0 and ch == 0:
                    g.wait_ge(S.get("cch" + uid), 1)
                    g.wait_ge(S.get("mcd"), t_mcd)
                    g.wait_ge(S.get("acc_ms"), t_ms_all)
                g.wait_ge(S.get("xtp"), need_xtp)
                g.wait_ge(S.get("xgd"), S.val("xgd"))
                g.indirect_dma_start(
                    out=tmrow2, out_offset=None, in_=U["hfull"].ap(),
                    in_offset=IndirectOffsetOnAxis(
                        ap=W["mapu"][:, e * MOE_CH + ch:e * MOE_CH + ch + 1],
                        axis=0),
                    bounds_check=B.rb_h, oob_is_err=False,
                ).then_inc(S.get("xgd"), 16)
            t_xg = S.bump("xgd", 16)

            @block.tensor
            def _(t, e=e, ch=ch, t_xg=t_xg, t_mo_all=t_mo_all):
                t.wait_ge(S.get("xgd"), t_xg)
                if e == 0 and ch < 2:
                    t.wait_ge(S.get("ev_mo"), t_mo_all)
                elif not (e == 0 and ch < 2):
                    t.wait_ge(S.get("xte"), S.val("xte") - 1)
                for g in range(G):
                    mm = t.transpose(psbf(2 + ch % 2)[:, g * 128:(g + 1) * 128],
                                     tmrow2[:, g * 128:(g + 1) * 128],
                                     W["identbf"][:])
                mm.then_inc(S.get("xtp"), 1)
            S.bump("xtp", 1)
            t_xt = S.val("xtp")

            @block.vector
            def _(v, e=e, ch=ch, t_xt=t_xt):
                v.wait_ge(S.get("xtp"), t_xt)
                v.tensor_copy(xef[:, :, ch * 128:(ch + 1) * 128],
                              psbf(2 + ch % 2)[:, 0:1024].rearrange(
                                  "p (g c) -> p g c", g=G)).then_inc(S.get("xte"), 1)
            S.bump("xte", 1)
        t_xe = S.val("xte")

        wf = [("xte", t_xe), ("ev_xo", S.val("ev_xo")), ("yte", S.val("yte"))]
        if e == 0:
            wf += [("ev_mo", t_mo_all), ("trd", t_ext), ("bsd", t_bsd2),
                   ("mcd", t_mcd), ("exd", S.val("exd")), ("shd", t_sh)]
        emit_pair_stage(block, nc, S, B, tag="xu", w1_dram=W["W1"].ap()[e],
                        w2_dram=W["W3"].ap()[e], rhs=xef, rhs_cols0=0,
                        fsubs=MOE_FS, n_og=32, n_kg=G, dst=hev,
                        silu_scr=exscr, wait_first=wf,
                        pan_guard=(("acc_ms", t_ms_all) if e == 0
                                   else ("acc_xu", S.val("acc_xu"))))
        t_heu = S.val("ev_xu")

        def wye(blk, og, fi, fo, fw, bank, t_acc2):
            @blk.scalar
            def _(s, og=og, fo=fo, fw=fw, bank=bank, t_acc2=t_acc2):
                s.wait_ge(S.get("acc_xo"), t_acc2)
                s.copy(yebf[:, og, fo:fo + fw], B.PS[:, bank, 0:fw]
                       ).then_inc(S.get("ev_xo"), 1)

        emit_single_stage(block, nc, S, B, tag="xo", w_dram=W["W2"].ap()[e],
                          rhs=hev, rhs_cols0=0, fsubs=MOE_FS, n_og=8, n_kg=32,
                          writer=wye, pan=B.panh, pan_nbuf=2,
                          wait_first=[("ev_xu", t_heu)],
                          pan_guard=(("acc_mo", S.val("acc_mo")) if e == 0
                                     else ("acc_xo", S.val("acc_xo"))),
                          bank_rot=2)
        t_ye = S.val("ev_xo")

        base_yt = S.val("ytp")
        base_ye2 = S.val("yte")
        base_yd = S.val("ydd")
        for ch in range(MOE_CH):
            @block.tensor
            def _(t, e=e, ch=ch, t_ye=t_ye):
                if ch == 0:
                    t.wait_ge(S.get("ev_xo"), t_ye)
                if ch >= 2:
                    t.wait_ge(S.get("yte"), base_ye2 + ch - 1)
                for g in range(G):
                    mm = t.transpose(psbf(4 + ch % 2)[:, g * 128:(g + 1) * 128],
                                     yebf[:, g, ch * 128:(ch + 1) * 128],
                                     W["identbf"][:])
                mm.then_inc(S.get("ytp"), 1)
            S.bump("ytp", 1)

            need_yd = S.val("ydd")

            @block.vector
            def _(v, e=e, ch=ch, need_yd=need_yd):
                v.wait_ge(S.get("ytp"), base_yt + ch + 1)
                v.wait_ge(S.get("ydd"), need_yd)
                if e == 0 and ch == 0:
                    v.wait_ge(S.get("hsd"), t_hs)
                v.tensor_copy(tmrow, psbf(4 + ch % 2)[:, 0:1024]).then_inc(S.get("yte"), 1)
            S.bump("yte", 1)
            t_yte = S.val("yte")

            t_ydw = S.val("ydd")

            @block.sync
            def _(q, e=e, ch=ch, t_yte=t_yte, t_ydw=t_ydw):
                q.wait_ge(S.get("yte"), t_yte)
                q.wait_ge(S.get("ydd"), t_ydw)
                q.dma_start(out=U["yeshard"].ap()[e * MOE_CAPC + ch * 128:
                                                  e * MOE_CAPC + (ch + 1) * 128, :],
                            in_=tmrow).then_inc(S.get("ydd"), 16)
            S.bump("ydd", 16)
    t_yed = S.val("ydd")


    @block.gpsimd
    def _(g, t_yed=t_yed):
        g.wait_ge(S.get("ydd"), t_yed)
        g.collective_compute(
            "AllGather", ALU.bypass, replica_groups=[list(range(NCORES))],
            ins=[U["yeshard"].ap().opt()], outs=[U["yeall"].ap().opt()],
        ).then_inc(S.get("ccy" + uid))
        g.wait_ge(S.get("ccy" + uid), 1)
        g.nop().then_inc(S.get("yag"), 1)
    S.bump("ccy" + uid, 1)
    t_yag = S.bump("yag", 1)

    @block.vector
    def _(v, t_yed=t_yed):
        v.wait_ge(S.get("ydd"), t_yed)
        vstep(v, S, v.memset(tmrow, 0.0))
        v.memset(tmrow2, 0.0).then_inc(S.get("tcz"), 1)
    t_tcz = S.bump("tcz", 1)

    base_cte = S.val("cte")
    base_ctp = S.val("ctp")
    base_tcb = S.val("tcb")
    for ch in range(16):
        need_tcb = S.val("tcb")
        t_g1 = S.bump("tg1", 16)
        t_g2 = S.bump("tg2", 16)

        @block.gpsimd
        def _(g, ch=ch, t_yag=t_yag, t_tcz=t_tcz, need_tcb=need_tcb):
            if ch == 0:
                g.wait_ge(S.get("yag"), t_yag)
                g.wait_ge(S.get("tcz"), t_tcz)
            g.wait_ge(S.get("tcb"), need_tcb)
            g.wait_ge(S.get("tg1"), S.val("tg1") - 16)
            g.indirect_dma_start(
                out=tmrow, out_offset=None, in_=U["yeall"].ap(),
                in_offset=IndirectOffsetOnAxis(
                    ap=W["fu32"][:, ch * 2:ch * 2 + 1], axis=0),
                bounds_check=B.rb_map, oob_is_err=False,
            ).then_inc(S.get("tg1"), 16)
            g.wait_ge(S.get("tg2"), S.val("tg2") - 16)
            g.indirect_dma_start(
                out=tmrow2, out_offset=None, in_=U["yeall"].ap(),
                in_offset=IndirectOffsetOnAxis(
                    ap=W["fu32"][:, ch * 2 + 1:ch * 2 + 2], axis=0),
                bounds_check=B.rb_map, oob_is_err=False,
            ).then_inc(S.get("tg2"), 16)

        @block.vector
        def _(v, ch=ch, t_g1=t_g1, t_g2=t_g2):
            v.wait_ge(S.get("tg1"), t_g1)
            v.wait_ge(S.get("tg2"), t_g2)
            if ch >= 1:
                v.wait_ge(S.get("ctp"), base_ctp + ch)
            vstep(v, S, v.tensor_scalar(rt32, tmrow, gtab[:, ch, 0:1], None,
                                        op0=ALU.mult))
            v.scalar_tensor_tensor(rt32, tmrow2, gtab[:, ch, 1:2], rt32,
                                   op0=ALU.mult, op1=ALU.add).then_inc(S.get("tcb"), 1)
        S.bump("tcb", 1)
        t_cb = S.val("tcb")

        @block.tensor
        def _(t, ch=ch, t_cb=t_cb, t_yed_=S.val("yte")):
            t.wait_ge(S.get("tcb"), t_cb)
            if ch == 0:
                t.wait_ge(S.get("yte"), t_yed_)
            if ch >= 2:
                t.wait_ge(S.get("cte"), base_cte + 8 * (ch - 1))
            bb = (ch % 2) * 2 + 4
            for g in range(G):
                bank = bb + g // 4
                mm = t.transpose(PS[:, bank, (g % 4) * 128:(g % 4) * 128 + 128],
                                 rt32[:, g * 128:(g + 1) * 128], W["identf"][:])
            mm.then_inc(S.get("ctp"), 1)
        S.bump("ctp", 1)
        t_ctp = S.val("ctp")

        @block.vector
        def _(v, ch=ch, t_ctp=t_ctp):
            v.wait_ge(S.get("ctp"), t_ctp)
            bb = (ch % 2) * 2 + 4
            for g in range(G):
                bank = bb + g // 4
                last = v.tensor_copy(routed[:, g, ch * 128:(ch + 1) * 128],
                                     PS[:, bank, (g % 4) * 128:(g % 4) * 128 + 128])
            last.then_inc(S.get("cte"), 8)
        S.bump("cte", 8)
    t_cmb = S.val("cte")

    @block.gpsimd
    def _(g, t_cmb=t_cmb, t_sh=t_sh):
        g.wait_ge(S.get("cte"), t_cmb)
        g.wait_ge(S.get("shd"), t_sh)
        g.wait_ge(S.get("rsh"), S.val("rsh"))
        g.dma_start(out=routed[:, :, :], in_=fm(U["shdram"].ap(), 0, TOK),
                    accum_op=ALU.add).then_inc(S.get("rsh"), 16)
    t_rsh = S.bump("rsh", 16)

    for fs in range(4):
        c0 = fs * FSUB
        need_xnd = S.val("xnd")
        need_out = S.val("outd")
        t_xld = S.bump("xld", 16)

        @block.sync
        def _(q, c0=c0, need_xnd=need_xnd, need_out=need_out, t_rsh=t_rsh,
              t_cb2=S.val("tcb"), t_xld=t_xld):
            q.wait_ge(S.get("xnd"), need_xnd)
            q.wait_ge(S.get("outd"), need_out)
            q.wait_ge(S.get("rsh"), t_rsh)
            q.wait_ge(S.get("tcb"), t_cb2)
            q.wait_ge(S.get("xld"), t_xld - 16)
            q.dma_start(out=B.xsub[:], in_=fm(x_dram, c0, FSUB)).then_inc(S.get("xld"), 16)

        @block.vector
        def _(v, c0=c0, t_xld=t_xld, t_rsh=t_rsh):
            v.wait_ge(S.get("xld"), t_xld)
            v.wait_ge(S.get("rsh"), t_rsh)
            for g in range(G):
                last = v.scalar_tensor_tensor(B.xsub[:, g, :],
                                              routed[:, g, c0:c0 + FSUB], 0.5,
                                              B.xsub[:, g, :],
                                              op0=ALU.mult, op1=ALU.add)
            last.then_inc(S.get("fod"), 1)
        t_fo = S.bump("fod", 1)

        t_odw = S.val("outd")

        @block.sync
        def _(q, c0=c0, t_fo=t_fo, t_odw=t_odw):
            q.wait_ge(S.get("fod"), t_fo)
            q.wait_ge(S.get("outd"), t_odw)
            q.dma_start(out=fm(xout_dram, c0, FSUB), in_=B.xsub[:]).then_inc(S.get("outd"), 16)
        S.bump("outd", 16)

    B.scr_guard = [("tg1", S.val("tg1")), ("tg2", S.val("tg2")),
                   ("trp", S.val("trp")), ("fod", S.val("fod"))]
    B.b03_last = [("ev_xo", S.val("ev_xo")), ("rv", t_rv)]
    B.b47_last = [("cte", t_cmb), ("mcd", t_mcd), ("exd", S.val("exd"))]
    B.pan_last["panu"] = ("acc_xu", S.val("acc_xu"))
    B.pan_last["panh"] = ("acc_xo", S.val("acc_xo"))


def declare_moe_shared(nc, stack, S, block, B, prefix="e_"):
    d = {}
    d["Wr"] = nc.dram_tensor(prefix + "Wr", [1024, 7], F32, kind="ExternalInput")
    d["W1"] = nc.dram_tensor(prefix + "W1", [7, 1024, 4096], F32, kind="ExternalInput")
    d["W3"] = nc.dram_tensor(prefix + "W3", [7, 1024, 4096], F32, kind="ExternalInput")
    d["W2"] = nc.dram_tensor(prefix + "W2", [7, 4096, 1024], F32, kind="ExternalInput")
    d["W1s"] = nc.dram_tensor(prefix + "W1s", [1024, 4096], F32, kind="ExternalInput").ap()
    d["W3s"] = nc.dram_tensor(prefix + "W3s", [1024, 4096], F32, kind="ExternalInput").ap()
    d["W2s"] = nc.dram_tensor(prefix + "W2s", [4096, 1024], F32, kind="ExternalInput").ap()
    ec = stack.enter_context
    for nm, ng in [("lng", G), ("lnb", G), ("miota", 7), ("mdesc", 7),
                   ("mcols", 2), ("mmask", 2), ("tokb", 16)]:
        d[nm + "_dram"] = nc.dram_tensor(prefix + nm, [128, ng], F32,
                                         kind="ExternalInput").ap()
        t = ec(nc.sbuf_tensor(prefix + nm + "_sb", [128, ng], F32))
        d[nm] = t

        tw = S.val("xld")

        @block.sync
        def _(q, t=t, src=d[nm + "_dram"], tw=tw):
            q.wait_ge(S.get("xld"), tw)
            q.dma_start(out=t[:], in_=src).then_inc(S.get("xld"), 16)
        S.bump("xld", 16)
    d["identbf_dram"] = nc.dram_tensor(prefix + "idbf", [128, 128], BF16,
                                       kind="ExternalInput").ap()
    d["identf_dram"] = nc.dram_tensor(prefix + "idf", [128, 128], F32,
                                      kind="ExternalInput").ap()
    d["identbf"] = ec(nc.sbuf_tensor(prefix + "idbf_sb", [128, 128], BF16))
    d["identf"] = ec(nc.sbuf_tensor(prefix + "idf_sb", [128, 128], F32))
    d["wr"] = ec(nc.sbuf_tensor(prefix + "wr_sb", [128, G, 7], F32))
    d["cnt"] = ec(nc.sbuf_tensor(prefix + "cnt", [128, 2], F32))
    d["cnta"] = ec(nc.sbuf_tensor(prefix + "cnta", [128, 16], F32))
    d["brow"] = ec(nc.sbuf_tensor(prefix + "brow", [1, 32], F32))
    d["bcol"] = ec(nc.sbuf_tensor(prefix + "bcol", [128, 2], F32))
    d["mapc"] = ec(nc.sbuf_tensor(prefix + "mapc", [128, 128], F32))
    d["mapcf"] = ec(nc.sbuf_tensor(prefix + "mapcf", [128, MOE_NCH], F32))
    d["mapu"] = ec(nc.sbuf_tensor(prefix + "mapu", [128, MOE_NCH], U32))
    d["fu32"] = ec(nc.sbuf_tensor(prefix + "fu32", [128, 32], U32))
    d["onescol"] = ec(nc.sbuf_tensor(prefix + "onescol", [128, 1], F32))
    d["onesrow"] = B.ones_row_f

    tw0 = S.val("xld")

    @block.sync
    def _(q, tw0=tw0):
        q.wait_ge(S.get("xld"), tw0)
        q.dma_start(out=d["identbf"][:], in_=d["identbf_dram"]).then_inc(S.get("xld"), 16)
        q.wait_ge(S.get("xld"), tw0 + 16)
        q.dma_start(out=d["identf"][:], in_=d["identf_dram"]).then_inc(S.get("xld"), 16)
    S.bump("xld", 32)

    @block.gpsimd
    def _(g):
        g.wait_ge(S.get("rwl"), S.val("rwl"))
        g.dma_start(out=d["wr"][:],
                    in_=d["Wr"].ap().rearrange("(g p) e -> p g e", p=128)
                    ).then_inc(S.get("rwl"), 16)
    S.bump("rwl", 16)

    @block.vector
    def _(v):
        v.memset(d["onescol"][:], 1.0).then_inc(S.get("init"), 1)
    S.bump("init", 1)
    return d


def declare_moe_scratch(nc, uid):
    U = {}
    U["hshard"] = nc.dram_tensor(uid + "hshard", [TOK, 1024], BF16)
    U["hfull"] = nc.dram_tensor(uid + "hfull", [TOK * NCORES, 1024], BF16)
    U["cntin"] = nc.dram_tensor(uid + "cntin", [7, 2], F32)
    U["cntall"] = nc.dram_tensor(uid + "cntall", [56, 2], F32)
    U["map"] = nc.dram_tensor(uid + "map", [MOE_ROWS, 1], F32)
    U["mapown"] = nc.dram_tensor(uid + "mapown", [MOE_SH, 1], F32)
    U["yeshard"] = nc.dram_tensor(uid + "yeshard", [MOE_SH, 1024], BF16)
    U["yeall"] = nc.dram_tensor(uid + "yeall", [MOE_ROWS, 1024], BF16)
    U["shdram"] = nc.dram_tensor(uid + "shdram", [1024, TOK], F32)
    return U


def moe_host_weights(inp, core):
    f = np.float32
    ident = np.eye(128)
    d = {
        "Wr": np.ascontiguousarray(np.asarray(inp["L1_ffn_Wr"], f)),
        "W1": np.ascontiguousarray(np.asarray(inp["L1_ffn_W1"], f)),
        "W3": np.ascontiguousarray(np.asarray(inp["L1_ffn_W3"], f)),
        "W2": np.ascontiguousarray(np.asarray(inp["L1_ffn_W2"], f)),
        "W1s": np.ascontiguousarray(np.asarray(inp["L1_ffn_W1s"], f)),
        "W3s": np.ascontiguousarray(np.asarray(inp["L1_ffn_W3s"], f)),
        "W2s": np.ascontiguousarray(np.asarray(inp["L1_ffn_W2s"], f)),
        "lng": pack_scale(inp["L1_ffn_ln_g"], G),
        "lnb": pack_scale(inp["L1_ffn_ln_b"], G),
        "idbf": np.ascontiguousarray(ident.astype(np.dtype("bfloat16")
                                                  if False else f)),
        "idf": np.ascontiguousarray(ident.astype(f)),
    }
    import ml_dtypes
    d["idbf"] = np.ascontiguousarray(ident.astype(ml_dtypes.bfloat16))
    p = np.arange(128, dtype=f)
    d["miota"] = np.ascontiguousarray(
        np.broadcast_to(np.arange(7, dtype=f), (128, 7)))
    d["mdesc"] = np.ascontiguousarray(
        np.broadcast_to(7.0 - np.arange(7, dtype=f), (128, 7)))
    mcols = np.zeros((128, 2), f)
    mcols[:, 0] = p
    mcols[:, 1] = float(MOE_CAPC) * p
    d["mcols"] = mcols
    mmask = np.zeros((128, 2), f)
    mmask[0:core, 0] = 1.0
    mmask[0:8, 1] = 1.0
    d["mmask"] = mmask
    tokb = np.zeros((128, 16), f)
    for ch in range(16):
        tokb[:, ch] = core * TOK + ch * 128 + p + 1.0
    d["tokb"] = tokb
    return d


def build_moe(debug=False):
    nc = bass.Bass("TRN2", target_bir_lowering=False, debug=debug)
    x = nc.dram_tensor("x", [D, TOK], F32, kind="ExternalInput").ap()
    xout = nc.dram_tensor("xout", [D, TOK], F32, kind="ExternalOutput").ap()
    with contextlib.ExitStack() as stack:
        S = Sems(nc, stack)
        B = Bufs(nc, stack)
        block = stack.enter_context(nc.Block())
        B.init_consts(block, S)
        W = declare_moe_shared(nc, stack, S, block, B)
        W["U"] = declare_moe_scratch(nc, "u0_")
        t_init = S.val("init")
        t_scales = S.val("xld")

        @block.tensor
        def _(t):
            t.wait_ge(S.get("init"), t_init)
            t.wait_ge(S.get("xld"), t_scales)

        @block.scalar
        def _(s):
            s.wait_ge(S.get("xld"), t_scales)

        @block.vector
        def _(v):
            v.wait_ge(S.get("xld"), t_scales)

        @block.gpsimd
        def _(g):
            g.wait_ge(S.get("xld"), t_scales)

        emit_moe(block, nc, S, B, W, x, xout, "u0")
        t_done = S.val("outd")

        @block.sync
        def _(q):
            q.wait_ge(S.get("outd"), t_done)
    return nc



_NCS = {}


def _get_ncs():
    if not _NCS:
        _NCS["ffn"] = build_l0ffn()
        _NCS["attn"] = build_l0attn()
        _NCS["moe"] = build_moe()
        _NCS["l1"] = build_l1attn()
    return _NCS


def kernel(**inp):
    ncs = _get_ncs()
    shards = fm_shards(inp["L0_attn_x"])

    Wf = l0ffn_host_weights(inp)
    ffn_keys = ["w1T", "w2T", "h1T", "h2T", "w3T", "s1", "s2", "sh1", "sh2", "s3"]
    attn_keys = ["fcT", "projT", "iw", "alog", "cout", "dskip", "fnw", "selcols"]
    moe_keys = ["Wr", "W1", "W3", "W2", "W1s", "W3s", "W2s", "lng", "lnb",
                "miota", "mdesc", "mcols", "mmask", "tokb", "idbf", "idf"]
    l1_keys = ["WoT", "gam", "bet", "ldt", "Dp", "bo", "selcols", "logA", "Bp", "Cp"]
    Wa = [l0attn_host_weights(inp, c) for c in range(NCORES)]
    Wm = [moe_host_weights(inp, c) for c in range(NCORES)]
    Wl = [l1attn_host_weights(inp, c) for c in range(NCORES)]

    def launch(nc, maps):
        res = run_bass_kernel_spmd(nc, maps, core_ids=list(range(NCORES)))
        return [r["xout"] for r in res.results]

    x = shards
    x = launch(ncs["ffn"], [dict(x=x[c], **{k: Wf[k] for k in ffn_keys})
                            for c in range(NCORES)])
    x = launch(ncs["attn"], [dict(x=x[c], **{"a_" + k: Wa[c][k] for k in attn_keys})
                             for c in range(NCORES)])
    x = launch(ncs["ffn"], [dict(x=x[c], **{k: Wf[k] for k in ffn_keys})
                            for c in range(NCORES)])
    x = launch(ncs["moe"], [dict(x=x[c], **{"e_" + k: Wm[c][k] for k in moe_keys})
                            for c in range(NCORES)])
    x = launch(ncs["l1"], [dict(x=x[c], **{"m_" + k: Wl[c][k] for k in l1_keys})
                           for c in range(NCORES)])
    x = launch(ncs["moe"], [dict(x=x[c], **{"e_" + k: Wm[c][k] for k in moe_keys})
                            for c in range(NCORES)])
    return fm_unshard(x).astype(np.float32)
